# version 29
# baseline (speedup 1.0000x reference)
import math
import numpy as np
import ml_dtypes
import concourse.bass as bass
import concourse.mybir as mybir
from concourse.bass_utils import run_bass_kernel_spmd

BF16 = mybir.dt.bfloat16
F32 = mybir.dt.float32
AF = mybir.ActivationFunctionType
ALU = mybir.AluOpType
AX = mybir.AxisListType
BF = ml_dtypes.bfloat16

ENGINES = ("sync", "tensor", "vector", "scalar", "gpsimd")


class Buf:
    __slots__ = ("h", "last_write", "readers", "name")

    def __init__(self, h, name=""):
        self.h = h
        self.name = name
        self.last_write = None
        self.readers = []

    def __getitem__(self, key):
        return self.h[key]

    def ap(self):
        return self.h[:]


class SubBuf:

    __slots__ = ("par", "off")

    def __init__(self, par, off):
        self.par = par
        self.off = off

    @property
    def h(self):
        return self.par.h

    @property
    def name(self):
        return self.par.name

    @property
    def last_write(self):
        return self.par.last_write

    @last_write.setter
    def last_write(self, v):
        self.par.last_write = v

    @property
    def readers(self):
        return self.par.readers

    @readers.setter
    def readers(self, v):
        self.par.readers = v

    def __getitem__(self, key):
        pk, ck = key
        c2 = slice((ck.start or 0) + self.off, ck.stop + self.off)
        return self.par.h[pk, c2]

    def ap(self):
        return self.par.h[:, self.off:self.off + 512]


class Prog:
    def __init__(self, nc):
        self.nc = nc
        self.steps = {e: [] for e in ENGINES}
        self.tick = {}
        self.tick_n = {}
        for e in ENGINES:
            self.tick[e] = nc.alloc_semaphore(f"tk_{e}")
            self.tick_n[e] = 0
        self.dma_ch = {}
        self.satisfied = {e: {} for e in ENGINES}
        self.n_waits = 0
        self.n_steps = 0

    def buf(self, h, name=""):
        return Buf(h, name)

    def sbuf(self, name, shape, dtype):
        return Buf(self.nc.alloc_sbuf_tensor(name, shape, dtype), name)

    def psum(self, name, shape, dtype=F32):
        return Buf(self.nc.alloc_psum_tensor(name, shape, dtype), name)

    def dram(self, name, shape, dtype, kind="Internal"):
        t = self.nc.dram_tensor(name, shape, dtype, kind=kind)
        return Buf(t.ap(), name)

    def dma_channel(self, name):
        if name not in self.dma_ch:
            self.dma_ch[name] = [self.nc.alloc_semaphore(f"dc_{name}"), 0]
        return name

    def _dep_tokens(self, reads, writes):
        toks = []
        for b in reads:
            if b.last_write is not None:
                toks.append(b.last_write)
        for b in writes:
            if b.last_write is not None:
                toks.append(b.last_write)
            toks.extend(b.readers)
        return toks

    def _emit_waits(self, engine_name, eng, toks, self_tok_limit):
        need = {}
        for sem, val in toks:
            if val > need.get(sem, -1):
                need[sem] = val
        sat = self.satisfied[engine_name]
        for sem, val in sorted(need.items(), key=lambda kv: kv[0].num):
            if sat.get(sem, 0) >= val:
                continue
            if sem is self.tick[engine_name] and val > self_tok_limit:
                raise RuntimeError(
                    f"deadlock: {engine_name} waits own future tick {val}")
            eng.wait_ge(sem, val)
            self.n_waits += 1
            sat[sem] = val

    def step(self, engine, fn, reads=(), writes=(), extra_deps=(), name=""):
        reads = [b for b in reads if b is not None]
        writes = [b for b in writes if b is not None]
        toks = self._dep_tokens(reads, writes) + list(extra_deps)
        sem = self.tick[engine]
        self.tick_n[engine] += 1
        myval = self.tick_n[engine]
        tok = (sem, myval)
        self_limit = myval - 1

        def run(eng, toks=toks, fn=fn):
            self._emit_waits(engine, eng, toks, self_limit)
            last = fn(eng)
            if last is None:
                last = eng.nop()
            last.then_inc(sem, 1)

        self.steps[engine].append(run)
        self.n_steps += 1
        for b in writes:
            b.last_write = tok
            b.readers = []
        for b in reads:
            b.readers.append(tok)
        return tok

    def dma_step(self, engine, channel, out_b, out_ap, in_b, in_ap,
                 extra_deps=(), name="", **dma_kwargs):
        writes = [b for b in (out_b if isinstance(out_b, (list, tuple))
                              else [out_b]) if b is not None]
        reads = [b for b in (in_b if isinstance(in_b, (list, tuple))
                             else [in_b]) if b is not None]
        if channel is None:
            channel = (writes + reads)[0].name
        channel = f"{channel}@{'sw' if engine == 'gpsimd' else 'hw'}"
        self.dma_channel(channel)
        ch = self.dma_ch[channel]
        sem = ch[0]
        prev = ch[1]
        ch[1] += 16
        tok = (sem, ch[1])
        toks = self._dep_tokens(reads, writes) + list(extra_deps)
        if prev > 0:
            toks.append((sem, prev))
        self_limit = self.tick_n[engine]

        def run(eng, toks=toks, out_ap=out_ap, in_ap=in_ap, kw=dma_kwargs):
            self._emit_waits(engine, eng, toks, self_limit)
            eng.dma_start(out=out_ap, in_=in_ap, **kw).then_inc(sem, 16)

        self.steps[engine].append(run)
        self.n_steps += 1
        for b in writes:
            b.last_write = tok
            b.readers = []
        for b in reads:
            b.readers.append(tok)
        return tok

    def finalize(self, final_engine="sync", final_deps=()):
        nc = self.nc
        with nc.Block() as block:
            for ename in ENGINES:
                steps = self.steps[ename]
                if not steps and ename != final_engine:
                    continue

                def body(eng, steps=steps, ename=ename):
                    for s in steps:
                        s(eng)
                    if ename == final_engine:
                        need = {}
                        for sem, val in final_deps:
                            if val > need.get(sem, -1):
                                need[sem] = val
                        sat = self.satisfied[ename]
                        for sem, val in sorted(need.items(),
                                               key=lambda kv: kv[0].num):
                            if sat.get(sem, 0) < val:
                                eng.wait_ge(sem, val)

                getattr(block, ename)(body)
        return nc


class Cfg:
    def __init__(self, D=4096, T=8, n_cores=8, blocks=("l0a", "l0f", "l1a", "l1f"),
                 sink_iters=50):
        self.D = D
        self.T = T
        self.HD = 128
        self.FC = D // 128
        self.H0 = self.FC
        self.H1 = self.FC
        self.HKV1 = max(1, self.FC // 4)
        self.DKV1 = self.HKV1 * 128
        self.GQ = self.H1 // self.HKV1
        self.NS = 4
        self.C = D // 4
        self.FCC = self.C // 128
        self.E, self.DK, self.R = 8, 32, 16
        self.ER = self.E * self.R
        self.n_cores = n_cores
        self.blocks = list(blocks)
        self.C0 = 8.0
        self.C1 = 30.0
        self.sink_iters = sink_iters
        self.G = 2 * T
        self.S = self.G * 128
        self.NT = T * 128
        self.NPAN = (self.NT + 511) // 512


CN = dict(negC0=0, negC1=1, scale0=2, invD=3, eps=4, half15=5,
          neghalf=6, rmseps=7, one=8, scaleDK=9, tau=10, negone=11)
NCONST = 12


def host_constants(cfg, core):
    par = core % 2
    consts = np.zeros((128, NCONST), np.float32)
    consts[:, CN["negC0"]] = -cfg.C0
    consts[:, CN["negC1"]] = -cfg.C1
    consts[:, CN["scale0"]] = 1.0 / math.sqrt(cfg.HD)
    consts[:, CN["invD"]] = 1.0 / cfg.D
    consts[:, CN["eps"]] = 1e-5
    consts[:, CN["half15"]] = 1.5
    consts[:, CN["neghalf"]] = -0.5
    consts[:, CN["rmseps"]] = 1.1920929e-07
    consts[:, CN["one"]] = 1.0
    consts[:, CN["scaleDK"]] = 1.0 / math.sqrt(cfg.DK)
    consts[0:16, CN["tau"]] = 10.0
    consts[16:, CN["tau"]] = 1.0
    consts[:, CN["negone"]] = -1.0
    idb = np.eye(128).astype(BF)
    idf = np.eye(128).astype(np.float32)
    onesb = np.ones((128, 1), BF)
    kk = np.arange(128)[:, None]
    qq = np.arange(128)[None, :]
    tri = (kk <= qq).astype(BF)
    mask_r0 = tri if par == 0 else np.ones((128, 128), BF)
    mask_r1 = np.zeros((128, 128), BF) if par == 0 else tri
    D_ = 128
    inv = 1.0 / (500000.0 ** (np.arange(0, D_, 2, dtype=np.float64) / D_))
    wav = 2 * math.pi / inv
    smooth = np.clip((8192.0 / wav - 1.0) / (4.0 - 1.0), 0.0, 1.0)
    newinv = (1 - smooth) * (inv / 8.0) + smooth * inv
    pos = np.concatenate([(2 * t + par) * 128 + np.arange(128)
                          for t in range(cfg.T)])
    fr = np.outer(pos.astype(np.float64), newinv)
    emb = np.concatenate([fr, fr], -1)
    cos_fm = np.ascontiguousarray(np.cos(emb).T.astype(BF))
    sin_fm = np.ascontiguousarray(np.sin(emb).T.astype(BF))
    ones_row_f = np.ones((1, 128), np.float32)
    prot = np.zeros((128, 128), np.float32)
    for dd in range(64):
        prot[dd + 64, dd] = -1.0
        prot[dd, dd + 64] = 1.0
    prot = prot.astype(BF)
    sel = np.zeros((cfg.E, cfg.ER), np.float32)
    for e in range(cfg.E):
        sel[e, e * cfg.R:(e + 1) * cfg.R] = 1.0
    return dict(consts=consts, idb=idb, idf=idf, onesb=onesb,
                mask_r0=mask_r0, mask_r1=mask_r1, cos_fm=cos_fm, sin_fm=sin_fm,
                sel=sel, ones_row_f=ones_row_f, prot=prot)


def tile_w(W, fc=None, ow=128):
    W = np.asarray(W, np.float32)
    Din, O = W.shape
    fc = fc if fc is not None else Din // 128
    assert Din == fc * 128 and O % ow == 0, (W.shape, fc, ow)
    ns = O // ow
    Wt = W.reshape(fc, 128, ns, ow).transpose(2, 1, 0, 3).reshape(ns, 128, fc * ow)
    return np.ascontiguousarray(Wt)


class Builder:
    def __init__(self, cfg):
        self.cfg = cfg
        self.preload_next = None
        self._y_seeded = False
        self.nc = bass.Bass("TRN2", target_bir_lowering=False)
        self.p = Prog(self.nc)
        self.final_deps = []

    def declare(self):
        cfg, p, nc = self.cfg, self.p, self.nc
        D, T, FC, NT = cfg.D, cfg.T, cfg.FC, cfg.NT

        def din(name, shape, dtype=F32):
            return p.buf(nc.dram_tensor(name, shape, dtype,
                                        kind="ExternalInput").ap(), name)

        i = self.i = {}
        i["x"] = din("x", [NT, D])
        i["consts"] = din("consts", [128, NCONST])
        i["idb"] = din("idb", [128, 128], BF16)
        i["idf"] = din("idf", [128, 128])
        i["onesb"] = din("onesb", [128, 1], BF16)
        i["ones_row_f"] = din("ones_row_f", [1, 128])
        i["prot"] = din("prot", [128, 128], BF16)
        i["mask_r0"] = din("mask_r0", [128, 128], BF16)
        i["mask_r1"] = din("mask_r1", [128, 128], BF16)
        i["cos_fm"] = din("cos_fm", [128, NT], BF16)
        i["sin_fm"] = din("sin_fm", [128, NT], BF16)
        i["sel"] = din("sel", [cfg.E, cfg.ER])
        i["w_qkv0"] = din("w_qkv0", [3 * FC, 128, FC * 128])
        i["b_qkv0"] = din("b_qkv0", [128, 3 * FC])
        i["w_o0"] = din("w_o0", [FC, 128, FC * 128])
        i["b_o0"] = din("b_o0", [128, FC])
        i["w_res0"] = din("w_res0", [1, 128, FC * 24])
        i["bases0"] = din("bases0", [24, 1])
        i["gate0"] = din("gate0", [1, 1])
        i["w_br0"] = din("w_br0", [cfg.FCC, 128, cfg.FCC * 128])
        i["w_q1"] = din("w_q1", [FC, 128, FC * 128])
        i["w_k1"] = din("w_k1", [cfg.HKV1, 128, FC * 128])
        i["w_v1"] = din("w_v1", [cfg.HKV1, 128, FC * 128])
        i["w_o1"] = din("w_o1", [FC, 128, FC * 128])
        i["w_W1"] = din("w_W1", [FC, 128, FC * 128])
        i["b_W1"] = din("b_W1", [128, FC])
        i["w_qk1"] = din("w_qk1", [1, 128, FC * 512])
        i["w_A1"] = din("w_A1", [1, 128, FC * 128])
        i["w_Bm1"] = din("w_Bm1", [FC, 128, 128])
        self.SLW = min(1024, D)
        self.NXCH = max(1, D // self.SLW)
        self.Yap = nc.dram_tensor("y", [NT, D], F32,
                                  kind="ExternalOutput").ap()
        i["Yt"] = [p.buf(self.Yap[t * 128:(t + 1) * 128, :], f"yt{t}")
                   for t in range(T)]

        d = self.d = {}
        d["q0"] = p.dram("dq0", [D, NT], BF16)
        d["k0own"] = p.dram("dk0own", [D, NT], BF16)
        d["k0full"] = p.dram("dk0full", [2 * D, NT], BF16)
        d["v0own"] = p.dram("dv0own", [FC * NT, 128], BF16)
        d["v0full"] = p.dram("dv0full", [2 * FC * NT, 128], BF16)
        d["q1"] = p.dram("dq1", [D, NT], BF16)
        d["k1own"] = p.dram("dk1own", [cfg.DKV1, NT], BF16)
        d["k1full"] = p.dram("dk1full", [2 * cfg.DKV1, NT], BF16)
        d["v1own"] = p.dram("dv1own", [cfg.HKV1 * NT, 128], BF16)
        d["v1full"] = p.dram("dv1full", [2 * cfg.HKV1 * NT, 128], BF16)
        for nm, n_h, ro, rf in [("k0", FC, 128, 256), ("v0", FC, NT, 2 * NT),
                                ("k1", cfg.HKV1, 128, 256),
                                ("v1", cfg.HKV1, NT, 2 * NT)]:
            d[nm + "ownH"] = [p.buf(d[nm + "own"].h, f"{nm}ow{h}")
                              for h in range(n_h)]
            d[nm + "fullH"] = [p.buf(d[nm + "full"].h, f"{nm}fl{h}")
                               for h in range(n_h)]
        d["lpk"] = p.dram("dlpk", [1, FC * NT], F32)
        d["lpk2"] = p.dram("dlpk2", [1, FC * NT], F32)

        W = self.SLW
        self.Xap = nc.dram_tensor("dX", [NT, D], F32).ap()
        d["Xc"] = [[p.buf(self.Xap, f"dX{t}c{q}") for q in range(self.NXCH)]
                   for t in range(T)]
        d["y0"] = p.dram("dy0", [cfg.FCC * NT, 128], BF16)
        self.actT = p.sbuf("actT", [128, FC * NT], BF16)
        self.wstg = [p.sbuf(f"wstg{j}", [128, max((FC // 8) * 128, 512)], F32)
                     for j in range(4)]
        self.wbf = [p.sbuf(f"wbf{j}", [128, FC * 128], BF16) for j in range(2)]
        self.stg = [p.sbuf(f"stg{j}", [128, max(NT, 1024)], BF16) for j in range(4)]
        self.xstage = [p.sbuf(f"xstg{j}", [128, D], F32) for j in range(2)]
        self.NXH = 2 * self.NXCH if self.SLW >= 2 else self.NXCH
        self.xsc = [[p.buf(self.xstage[j].h, f"xsc{j}q{q}")
                     for q in range(self.NXH)] for j in range(2)]
        self.stgF = p.sbuf("stgF", [128, max(NT, 1024)], F32)
        self.accb = [p.sbuf(f"accb{j}", [128, cfg.C], BF16) for j in range(2)]
        self.tbx = None
        self.ST_FS = 4 * T
        self.ST_RMS = 6 * T
        self.stats = p.sbuf("stats", [128, 11 * T], F32)
        self.stats2 = p.sbuf("stats2", [128, 11 * T], F32)
        self.stsc = [p.sbuf(f"stsc{j}", [128, 2 * self.NXCH], F32)
                     for j in range(2)]
        self.sc = p.sbuf("sc", [128, NCONST], F32)
        self.sid_b = p.sbuf("sid_b", [128, 128], BF16)
        self.sid_f = p.sbuf("sid_f", [128, 128], F32)
        self.sones = p.sbuf("sones", [128, 1], BF16)
        self.sonesr = p.sbuf("sonesr", [1, 128], F32)
        self.sprot = p.sbuf("sprot", [128, 128], BF16)
        self.smask0 = p.sbuf("smask0", [128, 128], BF16)
        self.smask1 = p.sbuf("smask1", [128, 128], BF16)
        self.scos = p.sbuf("scos", [128, NT], BF16)
        self.ssin = p.sbuf("ssin", [128, NT], BF16)

        self.ssel = p.sbuf("ssel", [cfg.E, cfg.ER], F32)
        self.sbias = p.sbuf("sbias", [128, 3 * FC], F32)
        self.sbases = p.sbuf("sbases", [24, 2], F32)
        self.sgate = p.sbuf("sgate", [1, 2], F32)
        self.aK = [p.sbuf(f"aK{j}", [128, 2 * NT], BF16) for j in range(2)]
        self.aV = [p.sbuf(f"aV{j}", [128, 2 * NT], BF16) for j in range(2)]
        self.aQ = [p.sbuf(f"aQ{j}", [128, NT], BF16) for j in range(2)]
        self.aE = [p.sbuf(f"aE{j}", [128, NT], BF16) for j in range(2)]
        self.tbx = self.aE
        self.lrows = p.sbuf("lrows", [128, max(NT, 1024)], F32)
        self.xpb = p.sbuf("xpb", [128, cfg.C], BF16)
        self.ytile = p.sbuf("ytile", [128, cfg.C], BF16)
        self.xpT = p.sbuf("xpT", [128, cfg.FCC * NT], BF16)
        self.lgt = p.sbuf("lgt", [24, 128], F32)
        self.Ltm = p.sbuf("Ltm", [128, T * 24], F32)
        TH = (T + 1) // 2
        self.sinkEh = [p.sbuf(f"sinkE{j}", [128, TH * 16], F32) for j in range(2)]
        self.sinkPh = [p.sbuf(f"sinkP{j}", [128, TH * 16], F32) for j in range(2)]
        self.sinkUh = [p.sbuf(f"sinkU{j}", [128, TH * 4], F32) for j in range(2)]
        self.sinkVh = [p.sbuf(f"sinkV{j}", [128, TH * 4], F32) for j in range(2)]
        self.sinkMh = [p.sbuf(f"sinkM{j}", [128, TH * 8], F32) for j in range(2)]
        self.sinkM = self.sinkMh[0]
        self.sinkU = self.sinkUh[0]
        self.sinkV = self.sinkVh[0]

        self.qk_tm = p.sbuf("qk_tm", [128, 2 * 512], BF16)
        self.w_tm = p.sbuf("w_tm", [128, T * 8], F32)
        self.wT = p.sbuf("wT", [8, NT], F32)
        self.h_fm = p.sbuf("h_fm", [128, NT], BF16)
        self.hp = p.sbuf("hp", [128, NT], BF16)
        self.bmt = [p.sbuf(f"bmt{j}", [128, 128], BF16) for j in range(2)]
        self.bmstg = p.sbuf("bmstg", [128, 128], F32)
        self.pq = [p.psum(f"pq{j}", [128, 1024], F32) for j in range(4)]
        self.pb = [SubBuf(self.pq[j // 2], (j % 2) * 512) for j in range(8)]

    def pbb(self, bank):
        v = self.pq[bank // 2].h[:].bitcast(BF16)
        return v[:, (bank % 2) * 1024:(bank % 2) * 1024 + 1024]

    def ccol(self, name):
        j = CN[name]
        return self.sc[:, j:j + 1]

    def load_inputs(self):
        p, cfg = self.p, self.cfg
        for j, (key, sb) in enumerate([
                ("consts", self.sc), ("idb", self.sid_b),
                ("idf", self.sid_f), ("onesb", self.sones),
                ("mask_r0", self.smask0), ("mask_r1", self.smask1),
                ("cos_fm", self.scos), ("sin_fm", self.ssin),
                ("sel", self.ssel),
                ("ones_row_f", self.sonesr), ("prot", self.sprot)]):
            p.dma_step("sync", f"ini{j % 4}", sb, sb.ap(), self.i[key],
                       self.i[key].ap())
        p.dma_step("sync", "ini3", self.sbases, self.sbases[:, 0:1],
                   self.i["bases0"], self.i["bases0"].ap())
        p.dma_step("sync", "ini2", self.sgate, self.sgate[:, 0:1],
                   self.i["gate0"], self.i["gate0"].ap())
        pass

    def load_xtile(self, t, src=None):
        xs = self.xstage[t % 2]
        H = self.cfg.D // self.NXH
        for q in range(self.NXH):
            c0 = q * H
            if src is None:
                inb = self.d["Xc"][t][c0 // self.SLW]
                inap = self.Xap[t * 128:(t + 1) * 128, c0:c0 + H]
            else:
                inb = src
                inap = src[t * 128:(t + 1) * 128, c0:c0 + H]
            self.p.dma_step("sync", f"xld{t % 2}q{q}", self.xsc[t % 2][q],
                            xs[:, c0:c0 + H], inb, inap)
        return xs

    def xr(self, t, lo, hi):
        W2 = self.cfg.D // self.NXH
        return [self.xsc[t % 2][q]
                for q in range(lo // W2, (hi - 1) // W2 + 1)]

    def emit_tile_sums(self, t, xs):
        cfg, p = self.cfg, self.p
        T, W = cfg.T, self.SLW
        st = self.stats
        nsl = self.NXCH
        ss = self.stsc[t % 2]
        for q in range(nsl):
            xb = self.xr(t, q * W, (q + 1) * W)
            if q % 2 == 0:
                p.step("vector", lambda v, q=q, ss=ss: v.reduce_sum(
                    out=ss[:, q:q + 1], in_=xs[:, q * W:(q + 1) * W],
                    axis=AX.X), reads=xb, writes=[ss])
            else:
                p.step("scalar", lambda s, q=q, ss=ss: s.activation(
                    out=self.stg[3][:, 0:W], in_=xs[:, q * W:(q + 1) * W],
                    func=AF.Identity, accum_out=ss[:, q:q + 1]),
                    reads=xb, writes=[ss, self.stg[3]])
            p.step("scalar", lambda s, q=q, ss=ss: s.activation(
                out=self.stg[3][:, 0:W], in_=xs[:, q * W:(q + 1) * W],
                func=AF.Square, accum_out=ss[:, nsl + q:nsl + q + 1]),
                reads=xb, writes=[ss, self.stg[3]])
        p.step("vector", lambda v, ss=ss: v.reduce_sum(
            out=st[:, t:t + 1], in_=ss[:, 0:nsl], axis=AX.X),
            reads=[ss], writes=[st])
        p.step("vector", lambda v, ss=ss: v.reduce_sum(
            out=st[:, T + t:T + t + 1], in_=ss[:, nsl:2 * nsl], axis=AX.X),
            reads=[ss], writes=[st])

    def emit_tile_finish(self, t, rms=False):
        cfg, p = self.cfg, self.p
        T = cfg.T
        st = self.stats
        FS = self.ST_FS
        mu = st[:, 2 * T + t:2 * T + t + 1]
        r = st[:, 3 * T + t:3 * T + t + 1]
        s1 = st[:, FS:FS + 1]
        s2 = st[:, FS + 1:FS + 2]
        var = st[:, FS + 2:FS + 3]
        p.step("vector", lambda v: v.tensor_scalar(
            out=mu, in0=st[:, t:t + 1], scalar1=self.ccol("invD"),
            scalar2=None, op0=ALU.mult), reads=[st, self.sc], writes=[st])
        p.step("vector", lambda v: v.tensor_mul(out=s1, in0=mu, in1=mu),
               reads=[st], writes=[st])
        p.step("vector", lambda v: v.scalar_tensor_tensor(
            out=var, in0=st[:, T + t:T + t + 1], scalar=self.ccol("invD"),
            in1=s1, op0=ALU.mult, op1=ALU.subtract),
            reads=[st, self.sc], writes=[st])
        p.step("vector", lambda v: v.tensor_scalar(
            out=r, in0=var, scalar1=self.ccol("eps"), scalar2=None,
            op0=ALU.add), reads=[st, self.sc], writes=[st])
        self._rsqrt(r, s1, s2)
        rs = st[:, 8 * T + t:8 * T + t + 1]
        nb = st[:, 9 * T + t:9 * T + t + 1]
        if rms:
            rr = st[:, 7 * T + t:7 * T + t + 1]
            p.step("vector", lambda v: v.tensor_mul(out=rr, in0=var, in1=r),
                   reads=[st], writes=[st])
            p.step("vector", lambda v: v.tensor_mul(out=rr, in0=rr, in1=r),
                   reads=[st], writes=[st])
            p.step("vector", lambda v: v.tensor_scalar(
                out=rr, in0=rr, scalar1=self.ccol("rmseps"), scalar2=None,
                op0=ALU.add), reads=[st, self.sc], writes=[st])
            self._rsqrt(rr, s1, s2)
            p.step("vector", lambda v: v.tensor_mul(out=rs, in0=r, in1=rr),
                   reads=[st], writes=[st])
            p.step("vector", lambda v: v.scalar_tensor_tensor(
                out=st[:, 10 * T + t:10 * T + t + 1], in0=mu,
                scalar=self.ccol("negone"), in1=r,
                op0=ALU.mult, op1=ALU.mult), reads=[st, self.sc], writes=[st])
        else:
            p.step("vector", lambda v: v.tensor_copy(out=rs, in_=r),
                   reads=[st], writes=[st])
        p.step("vector", lambda v: v.scalar_tensor_tensor(
            out=nb, in0=mu, scalar=self.ccol("negone"), in1=rs,
            op0=ALU.mult, op1=ALU.mult), reads=[st, self.sc], writes=[st])

    def emit_stats_finish(self):
        cfg, p = self.cfg, self.p
        T = cfg.T
        st = self.stats
        mu = st[:, 2 * T:3 * T]
        r = st[:, 3 * T:4 * T]
        scr = st[:, 4 * T:5 * T]
        p.step("vector", lambda v: v.tensor_scalar(
            out=mu, in0=st[:, 0:T], scalar1=self.ccol("invD"), scalar2=None,
            op0=ALU.mult), reads=[st, self.sc], writes=[st])
        p.step("vector", lambda v: v.tensor_mul(out=scr, in0=mu, in1=mu),
               reads=[st], writes=[st])
        p.step("vector", lambda v: v.scalar_tensor_tensor(
            out=r, in0=st[:, T:2 * T], scalar=self.ccol("invD"), in1=scr,
            op0=ALU.mult, op1=ALU.subtract),
            reads=[st, self.sc], writes=[st])
        p.step("vector", lambda v: v.tensor_scalar(
            out=r, in0=r, scalar1=self.ccol("eps"), scalar2=None, op0=ALU.add),
            reads=[st, self.sc], writes=[st])
        self._rsqrt(r, st[:, 4 * T:5 * T], st[:, 5 * T:6 * T])

    def _rsqrt(self, veps, scr1, scr2):
        p = self.p
        st = self.stats
        p.step("scalar", lambda s: s.activation(out=scr1, in_=veps, func=AF.Sqrt),
               reads=[st], writes=[st])
        p.step("vector", lambda v: v.reciprocal(out=scr2, in_=scr1),
               reads=[st], writes=[st])
        p.step("vector", lambda v: v.tensor_mul(out=scr1, in0=scr2, in1=scr2),
               reads=[st], writes=[st])
        p.step("vector", lambda v: v.tensor_mul(out=scr1, in0=scr1, in1=veps),
               reads=[st], writes=[st])
        p.step("vector", lambda v: v.tensor_scalar(
            out=scr1, in0=scr1, scalar1=self.ccol("neghalf"),
            scalar2=self.ccol("half15"), op0=ALU.mult, op1=ALU.add),
            reads=[st, self.sc], writes=[st])
        p.step("vector", lambda v: v.tensor_mul(out=veps, in0=scr2, in1=scr1),
               reads=[st], writes=[st])

    def norm_slice(self, t, xs, lo, hi, dst_ap, dst_buf, eng="vector",
                   plain=False):
        cfg, p = self.cfg, self.p
        T = cfg.T
        st = self.stats
        sc_c = 3 * T + t if plain else 8 * T + t
        nb_c = 10 * T + t if plain else 9 * T + t
        if eng == "vector":
            p.step("vector", lambda v: v.tensor_scalar(
                out=dst_ap, in0=xs[:, lo:hi],
                scalar1=st[:, 2 * T + t:2 * T + t + 1],
                scalar2=st[:, sc_c:sc_c + 1],
                op0=ALU.subtract, op1=ALU.mult),
                reads=self.xr(t, lo, hi) + [st], writes=[dst_buf])
        else:
            p.step("scalar", lambda s: s.activation(
                out=dst_ap, in_=xs[:, lo:hi], func=AF.Identity,
                scale=st[:, sc_c:sc_c + 1],
                bias=st[:, nb_c:nb_c + 1]),
                reads=self.xr(t, lo, hi) + [st], writes=[dst_buf])

    def transpose_to(self, src, src_off, nchunks, dstbuf, dst_fn):
        p = self.p
        for cg in range(0, nchunks, 8):
            n = min(8, nchunks - cg)
            bank = 4 + 2 * ((cg // 8) % 2)

            def tf(te, cg=cg, n=n, bank=bank):
                last = None
                for k in range(n):
                    last = te.transpose(
                        out=self.pbb(bank)[:, k * 128:(k + 1) * 128],
                        in_=src[:, src_off + (cg + k) * 128:
                                src_off + (cg + k + 1) * 128],
                        identity=self.sid_b[:])
                return last
            p.step("tensor", tf, reads=[src, self.sid_b],
                   writes=[self.pb[bank]])
            p.step("vector", lambda v, cg=cg, n=n, bank=bank: v.tensor_copy(
                out=dst_fn(cg, n),
                in_=self.pbb(bank)[:, 0:n * 128].rearrange(
                    "p (k j) -> p k j", j=128)),
                reads=[self.pb[bank]], writes=[dstbuf])

    def emit_ln_to_actT(self, src=None):
        for t in range(self.cfg.T):
            xs = self.load_xtile(t, src)
            self.emit_ln_tile(t, xs)

    def emit_ln_tile(self, t, xs, stgs=None):
        cfg = self.cfg
        NT = cfg.NT
        W = self.SLW
        nch = W // 128
        self.emit_tile_sums(t, xs)
        self.emit_tile_finish(t)
        for q in range(cfg.D // W):
            sg = (stgs or self.stg)[q % 4]
            self.norm_slice(t, xs, q * W, (q + 1) * W, sg[:, 0:W], sg,
                            eng="scalar" if q == 1 else "vector")
            self.transpose_to(
                sg, 0, nch, self.actT,
                lambda cg, n, t=t, q=q: self.actT[:].rearrange(
                    "p (c n) -> p c n", n=NT)[
                    :, nch * q + cg:nch * q + cg + n,
                    t * 128:(t + 1) * 128])

    def stream_w(self, wdram, s, fc, ow, wslot):
        p = self.p
        ns = min(8, fc)
        for i in range(ns):
            c0 = i * fc // ns
            c1 = (i + 1) * fc // ns
            if c0 >= c1:
                continue
            stgb = self.wstg[i % 4]
            p.dma_step("sync", None, stgb, stgb[:, 0:(c1 - c0) * ow],
                       wdram, wdram[s, :, c0 * ow:c1 * ow])
            p.step("vector", lambda v, stgb=stgb, c0=c0, c1=c1, wslot=wslot,
                   ow=ow: v.tensor_copy(out=self.wbf[wslot][:, c0 * ow:c1 * ow],
                                        in_=stgb[:, 0:(c1 - c0) * ow]),
                   reads=[stgb], writes=[self.wbf[wslot]])
        return self.wbf[wslot]

    def preload_w(self, wdram, nslices, ow=128, fc=None, s_base=0):
        FC = fc if fc is not None else self.cfg.FC
        for s in range(s_base, min(s_base + 2, s_base + nslices)):
            self.stream_w(wdram, s, FC, ow, s % 2)

    def emit_wproj(self, tag, wdram, nslices, evac, moving_ap_fn=None,
                   moving_bufs=None, ow=128, fc=None, extra_mm=None,
                   extra_reads=(), pre_slice=None, s_base=0, preloaded=False):
        cfg, p = self.cfg, self.p
        FC = fc if fc is not None else cfg.FC
        NT = cfg.NT
        if moving_ap_fn is None:
            def moving_ap_fn(c, p0, p1):
                return self.actT[:, c * NT + p0: c * NT + p1]
        if moving_bufs is None:
            moving_bufs = [self.actT]
        pend = None
        for s in range(s_base, s_base + nslices):
            if s == s_base:
                if not preloaded:
                    self.stream_w(wdram, s, FC, ow, s % 2)
                if pre_slice is not None:
                    pre_slice(s)
            if s + 1 < s_base + nslices:
                if not (preloaded and s + 1 == s_base + 1):
                    self.stream_w(wdram, s + 1, FC, ow, (s + 1) % 2)
                if pre_slice is not None:
                    pre_slice(s + 1)
            wb = self.wbf[s % 2]
            bA = self.pb[2 * (s % 2)]
            bB = self.pb[2 * (s % 2) + 1]

            def mmf(te, wb=wb, s=s, bA=bA, bB=bB):
                last = None
                for c in range(FC):
                    lhsT = wb[:, c * ow:(c + 1) * ow]
                    for pan in range(cfg.NPAN):
                        p0, p1 = pan * 512, min(NT, (pan + 1) * 512)
                        bank = (bA, bB)[pan]
                        last = te.matmul(out=bank[0:ow, 0:p1 - p0], lhsT=lhsT,
                                         rhs=moving_ap_fn(c, p0, p1),
                                         start=(c == 0),
                                         stop=(c == FC - 1 and extra_mm is None))
                if extra_mm is not None:
                    last = extra_mm(te, s, bA, bB)
                return last
            p.step("tensor", mmf,
                   reads=[wb] + list(moving_bufs) + list(extra_reads),
                   writes=[bA, bB])
            if pend is not None:
                evac(*pend)
            pend = (s, bA, bB)
        if pend is not None:
            evac(*pend)

    def emit_qkv(self, layer):
        cfg, p = self.cfg, self.p
        FC, NT, T = cfg.FC, cfg.NT, cfg.T
        if layer == 0:
            p.dma_step("sync", None, self.sbias, self.sbias[:, 0:3 * FC],
                       self.i["b_qkv0"], self.i["b_qkv0"].ap())

            def evac(s, bA, bB):
                kind = s // FC
                h = s % FC
                stg = self.stg[0] if kind < 2 else self.stg[1]
                for pan in range(cfg.NPAN):
                    p0, p1 = pan * 512, min(NT, (pan + 1) * 512)
                    bank = (bA, bB)[pan]
                    p.step("scalar", lambda sc, bank=bank, stg=stg, p0=p0, p1=p1,
                           s=s: sc.activation(
                               out=stg[:, p0:p1], in_=bank[:, 0:p1 - p0],
                               func=AF.Identity, bias=self.sbias[:, s:s + 1]),
                           reads=[bank, self.sbias], writes=[stg])
                if kind == 0:
                    dst = self.d["q0"]
                    p.dma_step("scalar", None, dst, dst[h * 128:(h + 1) * 128, :],
                               stg, stg[:, 0:NT])
                elif kind == 1:
                    dst = self.d["k0own"]
                    p.dma_step("scalar", f"kst{h % 2}", self.d["k0ownH"][h],
                               dst[h * 128:(h + 1) * 128, :],
                               stg, stg[:, 0:NT])
                    self.emit_gather_head("k0", h, 128, 256)
                else:
                    self.v_transpose_store(stg, h, self.d["v0own"],
                                           out_buf=self.d["v0ownH"][h],
                                           ch=f"vst{h % 2}")
                    self.emit_gather_head("v0", h, NT, 2 * NT)
            self.emit_wproj("qkv0a", self.i["w_qkv0"], 2 * FC, evac,
                            preloaded=True)
            self.emit_wproj("qkv0b", self.i["w_qkv0"], FC, evac, s_base=2 * FC)
        else:
            self.emit_wproj("q1", self.i["w_q1"], FC,
                            lambda s, bA, bB: self.rope_evac(s, bA, bB,
                                                             self.d["q1"]),
                            preloaded=getattr(self, "_l1a_ready", False))
            def evack(s, bA, bB):
                self.rope_evac(s, bA, bB, self.d["k1own"],
                               out_buf=self.d["k1ownH"][s], ch=f"kst{s % 2}")
                self.emit_gather_head("k1", s, 128, 256)
            self.emit_wproj("k1", self.i["w_k1"], cfg.HKV1, evack)

            def evacv(s, bA, bB):
                stg = self.stg[1]
                for pan in range(cfg.NPAN):
                    p0, p1 = pan * 512, min(NT, (pan + 1) * 512)
                    bank = (bA, bB)[pan]
                    p.step("scalar", lambda sc, bank=bank, p0=p0, p1=p1:
                           sc.copy(out=stg[:, p0:p1], in_=bank[:, 0:p1 - p0]),
                           reads=[bank], writes=[stg])
                self.v_transpose_store(stg, s, self.d["v1own"],
                                       out_buf=self.d["v1ownH"][s],
                                       ch=f"vst{s % 2}")
                self.emit_gather_head("v1", s, NT, 2 * NT)
            self.emit_wproj("v1", self.i["w_v1"], cfg.HKV1, evacv)

    def rope_evac(self, h, bA, bB, dst, out_buf=None, ch=None):
        cfg, p = self.cfg, self.p
        NT = cfg.NT
        stgP, stgA, stgC = self.stg[0], self.stg[1], self.stg[2]
        rotA, rotB = self.pb[4], self.pb[5]
        for pan in range(cfg.NPAN):
            p0, p1 = pan * 512, min(NT, (pan + 1) * 512)
            bank = (bA, bB)[pan]
            p.step("scalar", lambda sc, bank=bank, p0=p0, p1=p1:
                   sc.copy(out=stgP[:, p0:p1], in_=bank[:, 0:p1 - p0]),
                   reads=[bank], writes=[stgP])
        def rmm(te):
            last = None
            for pan in range(cfg.NPAN):
                p0, p1 = pan * 512, min(NT, (pan + 1) * 512)
                bank = (rotA, rotB)[pan]
                last = te.matmul(out=bank[:, 0:p1 - p0], lhsT=self.sprot[:],
                                 rhs=stgP[:, p0:p1], start=True, stop=True)
            return last
        p.step("tensor", rmm, reads=[stgP, self.sprot], writes=[rotA, rotB])
        p.step("gpsimd", lambda g: g.tensor_mul(
            out=stgC[:, 0:NT], in0=stgP[:, 0:NT], in1=self.scos[:, 0:NT]),
            reads=[stgP, self.scos], writes=[stgC])
        for pan in range(cfg.NPAN):
            p0, p1 = pan * 512, min(NT, (pan + 1) * 512)
            bank = (rotA, rotB)[pan]
            p.step("vector", lambda v, bank=bank, p0=p0, p1=p1: v.tensor_mul(
                out=stgA[:, p0:p1], in0=bank[:, 0:p1 - p0],
                in1=self.ssin[:, p0:p1]),
                reads=[bank, self.ssin], writes=[stgA])
        p.step("vector", lambda v: v.tensor_add(
            out=stgC[:, 0:NT], in0=stgC[:, 0:NT], in1=stgA[:, 0:NT]),
            reads=[stgC, stgA], writes=[stgC])
        p.dma_step("scalar", ch, out_buf if out_buf is not None else dst,
                   dst[h * 128:(h + 1) * 128, :],
                   stgC, stgC[:, 0:NT])

    def v_transpose_store(self, stg, h, dst, out_buf=None, ch=None):
        cfg, p = self.cfg, self.p
        T, NT = cfg.T, cfg.NT
        stgT = self.stg[3]
        self.transpose_to(stg, 0, T, stgT,
                          lambda k0, n: stgT[:, k0 * 128:(k0 + n) * 128].rearrange(
                              "p (k j) -> p k j", j=128))
        p.dma_step("scalar", ch, out_buf if out_buf is not None else dst,
                   dst[h * NT:(h + 1) * NT, :].rearrange(
                       "(t p) j -> p t j", p=128),
                   stgT, stgT[:, 0:NT].rearrange("p (t j) -> p t j", j=128))

    def emit_gather_head(self, nm, h, ro, rf):
        p = self.p
        rg = [[c, c + 1] for c in range(0, self.cfg.n_cores, 2)]
        srcb = self.d[nm + "ownH"][h]
        dstb = self.d[nm + "fullH"][h]
        src_ap = self.d[nm + "own"].h[h * ro:(h + 1) * ro, :]
        dst_ap = self.d[nm + "full"].h[h * rf:(h + 1) * rf, :]
        p.step("gpsimd", lambda g, src_ap=src_ap, dst_ap=dst_ap:
               g.collective_compute(
                   "AllGather", ALU.bypass, replica_groups=rg,
                   ins=[src_ap], outs=[dst_ap]),
               reads=[srcb], writes=[dstb])

    def emit_gather_one(self, a, b):
        p = self.p
        rg = [[c, c + 1] for c in range(0, self.cfg.n_cores, 2)]
        srcb, dstb = self.d[a], self.d[b]
        p.step("gpsimd", lambda g, srcb=srcb, dstb=dstb: g.collective_compute(
            "AllGather", ALU.bypass, replica_groups=rg,
            ins=[srcb.ap()], outs=[dstb.ap()]),
            reads=[srcb], writes=[dstb])

    def emit_attention(self, layer):
        cfg, p = self.cfg, self.p
        T, NT, FC = cfg.T, cfg.NT, cfg.FC
        causal = (layer == 1)
        nkv = cfg.HKV1 if layer else cfg.H0
        gq = (cfg.H1 // cfg.HKV1) if layer else 1
        nheads = nkv * gq
        kfull = self.d["k1full" if layer else "k0full"]
        vfull = self.d["v1full" if layer else "v0full"]
        kfH = self.d[("k1" if layer else "k0") + "fullH"]
        vfH = self.d[("v1" if layer else "v0") + "fullH"]
        qd = self.d["q1" if layer else "q0"]
        DKV = cfg.DKV1 if layer else cfg.D
        negC = self.ccol("negC1" if layer else "negC0")
        masks = {0: self.smask0, 1: self.smask1}

        for kv in range(nkv):
            kb = self.aK[kv % 2]
            vb = self.aV[kv % 2]
            for r in range(2):
                p.dma_step("sync", None, kb, kb[:, r * NT:(r + 1) * NT],
                           kfH[kv], kfull[kv * 256 + r * 128:
                                          kv * 256 + (r + 1) * 128, :])
                p.dma_step("sync", None, vb,
                           vb[:, r * NT:(r + 1) * NT].rearrange(
                               "p (t j) -> p t j", j=128),
                           vfH[kv], vfull[(kv * 2 + r) * NT:
                                          (kv * 2 + r + 1) * NT, :].rearrange(
                               "(t p) j -> p t j", p=128))
            for qi in range(gq):
                h = kv * gq + qi
                qb = self.aQ[h % 2]
                p.dma_step("sync", None, qb, qb.ap(),
                           qd, qd[h * 128:(h + 1) * 128, :])
                ctxA, ctxB = self.pb[4], self.pb[5]
                lA, lB = self.pb[6], self.pb[7]
                slots = []
                for m in range(T):
                    for r in range(2):
                        qoff = m * 128 if causal else 0
                        if qoff < NT:
                            slots.append((r, m, qoff))
                pan_slots = {pan: [] for pan in range(cfg.NPAN)}
                for si, (r, m, qoff) in enumerate(slots):
                    for pan in range(cfg.NPAN):
                        p0 = max(pan * 512, qoff)
                        p1 = min(NT, (pan + 1) * 512)
                        if p0 < p1:
                            pan_slots[pan].append(si)

                def emit_smm(si):
                    r, m, qoff = slots[si]
                    sbA = self.pb[2 * (si % 2)]
                    sbB = self.pb[2 * (si % 2) + 1]
                    sl = r * T + m

                    def smm(te, sl=sl, sbA=sbA, sbB=sbB, qoff=qoff,
                            kb=kb, qb=qb):
                        last = None
                        for pan in range(cfg.NPAN):
                            p0 = max(pan * 512, qoff)
                            p1 = min(NT, (pan + 1) * 512)
                            if p0 >= p1:
                                continue
                            sb = (sbA, sbB)[pan]
                            last = te.matmul(
                                out=sb[:, p0 - pan * 512: p1 - pan * 512],
                                lhsT=kb[:, sl * 128:(sl + 1) * 128],
                                rhs=qb[:, p0:p1], start=True, stop=True)
                        return last
                    p.step("tensor", smm, reads=[kb, qb], writes=[sbA, sbB])

                def emit_tail(si):
                    r, m, qoff = slots[si]
                    spq = self.pq[si % 2]
                    eb = self.aE[si % 2]
                    sl = r * T + m
                    p.step("scalar", lambda sc, spq=spq, eb=eb, qoff=qoff:
                           sc.activation(
                               out=eb[:, qoff:NT], in_=spq.h[:, qoff:NT],
                               func=AF.Exp, bias=negC,
                               scale=self.ccol("scale0")),
                           reads=[spq, self.sc], writes=[eb])
                    if causal:
                        mk = masks[r]
                        p.step("vector", lambda v, eb=eb, m=m, mk=mk: v.tensor_mul(
                            out=eb[:, m * 128:(m + 1) * 128],
                            in0=eb[:, m * 128:(m + 1) * 128], in1=mk[:]),
                            reads=[eb, mk], writes=[eb])

                    def pvmm(te, si=si, sl=sl, eb=eb, qoff=qoff, vb=vb):
                        last = None
                        for lmm in range(2):
                            for pan in range(cfg.NPAN):
                                p0 = max(pan * 512, qoff)
                                p1 = min(NT, (pan + 1) * 512)
                                if p0 >= p1:
                                    continue
                                fs = (si == pan_slots[pan][0])
                                st_ = (si == pan_slots[pan][-1])
                                if lmm == 0:
                                    ctx = (ctxA, ctxB)[pan]
                                    last = te.matmul(
                                        out=ctx[:, p0 - pan * 512:
                                                p1 - pan * 512],
                                        lhsT=vb[:, sl * 128:(sl + 1) * 128],
                                        rhs=eb[:, p0:p1], start=fs, stop=st_)
                                else:
                                    lb_ = (lA, lB)[pan]
                                    last = te.matmul(
                                        out=lb_[0:1, p0 - pan * 512:
                                                p1 - pan * 512],
                                        lhsT=self.sones[:], rhs=eb[:, p0:p1],
                                        start=fs, stop=st_)
                        return last
                    p.step("tensor", pvmm, reads=[eb, vb, self.sones],
                           writes=[ctxA, ctxB, lA, lB])

                emit_smm(0)
                for si in range(len(slots)):
                    if si + 1 < len(slots):
                        emit_smm(si + 1)
                    emit_tail(si)
                p.step("vector", lambda v, h=h: v.tensor_copy(
                    out=self.actT[:, h * NT: h * NT + NT],
                    in_=self.pq[2].h[:, 0:NT]),
                    reads=[ctxA, ctxB], writes=[self.actT])
                p.step("vector", lambda v: v.tensor_copy(
                    out=self.lrows[0:1, 0:NT], in_=self.pq[3].h[0:1, 0:NT]),
                    reads=[lA, lB], writes=[self.lrows])
                p.dma_step("scalar", None, self.d["lpk"],
                           self.d["lpk"][0:1, h * NT:(h + 1) * NT],
                           self.lrows, self.lrows[0:1, 0:NT])
        p.dma_step("scalar", None, self.lrows, self.lrows[0:nheads, 0:NT],
                   self.d["lpk"], self.d["lpk"][0:1, 0:nheads * NT].rearrange(
                       "p (h n) -> (p h) n", n=NT))
        p.step("vector", lambda v: v.reciprocal(
            out=self.lrows[0:nheads, 0:NT], in_=self.lrows[0:nheads, 0:NT]),
            reads=[self.lrows], writes=[self.lrows])
        p.dma_step("scalar", None, self.d["lpk2"],
                   self.d["lpk2"][0:1, 0:nheads * NT].rearrange(
                       "p (h n) -> (p h) n", n=NT),
                   self.lrows, self.lrows[0:nheads, 0:NT])
        for h in range(nheads):
            p.dma_step("scalar", None, self.lrows, self.lrows[0:1, 0:NT],
                       self.d["lpk2"],
                       self.d["lpk2"][0:1, h * NT:(h + 1) * NT])
            bkA = self.pb[2 * (h % 2)]
            bkB = self.pb[2 * (h % 2) + 1]

            def bc(te, bkA=bkA, bkB=bkB):
                last = None
                for pan in range(cfg.NPAN):
                    p0, p1 = pan * 512, min(NT, (pan + 1) * 512)
                    bk = (bkA, bkB)[pan]
                    last = te.matmul(out=bk[:, 0:p1 - p0], lhsT=self.sonesr[:],
                                     rhs=self.lrows[0:1, p0:p1],
                                     start=True, stop=True)
                return last
            p.step("tensor", bc, reads=[self.lrows, self.sonesr],
                   writes=[bkA, bkB])
            p.step("vector", lambda v, h=h: v.tensor_mul(
                out=self.actT[:, h * NT: h * NT + NT],
                in0=self.actT[:, h * NT: h * NT + NT],
                in1=self.pq[h % 2].h[:, 0:NT]),
                reads=[self.actT, bkA, bkB], writes=[self.actT])

    def emit_outproj(self, tag, wdram, bias_dram, extra_mm=None,
                     extra_reads=(), pre_slice=None, dst="X"):
        cfg, p = self.cfg, self.p
        T, NT, FC = cfg.T, cfg.NT, cfg.FC
        if bias_dram is not None:
            p.dma_step("sync", None, self.sbias, self.sbias[:, 0:FC],
                       bias_dram, bias_dram.ap())

        def evac(s, bA, bB):
            stgE = self.stg[2]
            stgf = (self.stgF, self.lrows)[s % 2]
            for pan in range(cfg.NPAN):
                p0, p1 = pan * 512, min(NT, (pan + 1) * 512)
                bank = (bA, bB)[pan]
                if bias_dram is not None:
                    p.step("scalar", lambda sc, bank=bank, p0=p0, p1=p1, s=s:
                           sc.activation(out=stgE[:, p0:p1],
                                         in_=bank[:, 0:p1 - p0], func=AF.Identity,
                                         bias=self.sbias[:, s:s + 1]),
                           reads=[bank, self.sbias], writes=[stgE])
                else:
                    p.step("scalar", lambda sc, bank=bank, p0=p0, p1=p1:
                           sc.copy(out=stgE[:, p0:p1], in_=bank[:, 0:p1 - p0]),
                           reads=[bank], writes=[stgE])
            for tg in range(0, T, 8):
                n = min(8, T - tg)
                bank = 4 + 2 * ((tg // 8) % 2)

                def tf(te, tg=tg, n=n, bank=bank):
                    last = None
                    for k in range(n):
                        last = te.transpose(
                            out=self.pbb(bank)[:, k * 128:(k + 1) * 128],
                            in_=stgE[:, (tg + k) * 128:(tg + k + 1) * 128],
                            identity=self.sid_b[:])
                    return last
                p.step("tensor", tf, reads=[stgE, self.sid_b],
                       writes=[self.pb[bank]])
                p.step("vector", lambda v, bank=bank, n=n, stgf=stgf:
                       v.tensor_copy(
                           out=stgf[:, 0:n * 128],
                           in_=self.pbb(bank)[:, 0:n * 128]),
                       reads=[self.pb[bank]], writes=[stgf])
                chunk = (s * 128) // self.SLW
                tgts = []
                if dst in ("X", "Xy"):
                    tgts.append(([self.d["Xc"][tg + k][chunk]
                                  for k in range(n)],
                                 self.Xap, f"xacc{chunk}", False))
                if dst in ("y", "Xy"):
                    tgts.append(([self.i["Yt"][tg + k] for k in range(n)],
                                 self.Yap, f"yacc{chunk}", True))
                for ob, oap, cname, is_y in tgts:
                    tok = p.dma_step(
                        "gpsimd", cname, ob,
                        oap[:].rearrange("(t p) d -> p t d", p=128)[
                            :, tg:tg + n, s * 128:(s + 1) * 128],
                        stgf,
                        stgf[:, 0:n * 128].rearrange(
                            "p (k j) -> p k j", j=128),
                        accum_op=ALU.add)
                    if is_y:
                        self.final_deps.append(tok)
        self.emit_wproj(tag, wdram, FC, evac, extra_mm=extra_mm,
                        extra_reads=extra_reads, pre_slice=pre_slice)

    def emit_l0ffn(self, fuse_next_ln=False):
        cfg, p = self.cfg, self.p
        T, NT, FC, C, FCC = cfg.T, cfg.NT, cfg.FC, cfg.C, cfg.FCC
        D = cfg.D
        st = self.stats
        p.step("vector", lambda v: v.tensor_mul(
            out=self.sbases[:, 1:2], in0=self.sbases[:, 0:1],
            in1=self.sc[0:24, CN["tau"]:CN["tau"] + 1]),
            reads=[self.sbases, self.sc], writes=[self.sbases])
        p.step("scalar", lambda s: s.activation(
            out=self.sgate[:, 1:2], in_=self.sgate[:, 0:1], func=AF.Sigmoid),
            reads=[self.sgate], writes=[self.sgate])
        WS = 512
        for j in range((FC * 24 + WS - 1) // WS):
            c0, c1 = j * WS, min(FC * 24, (j + 1) * WS)
            stgb = self.wstg[j % 4]
            p.dma_step("sync", None, stgb, stgb[:, 0:c1 - c0],
                       self.i["w_res0"], self.i["w_res0"][0, :, c0:c1])
            p.step("scalar", lambda sc, stgb=stgb, c0=c0, c1=c1: sc.copy(
                out=self.wbf[0][:, c0:c1], in_=stgb[:, 0:c1 - c0]),
                reads=[stgb], writes=[self.wbf[0]])
        W = self.SLW
        nsl = D // W
        nch = W // 128
        for t in range(T):
            xs = self.load_xtile(t)
            self.emit_tile_sums(t, xs)
            self.emit_tile_finish(t, rms=True)
            for q in range(nsl):
                sg = self.stg[q % 4]
                self.norm_slice(t, xs, q * W, (q + 1) * W, sg[:, 0:W],
                                sg, eng="scalar" if q == 1 else "vector")
            for q in range(nsl):
                sg = self.stg[q % 4]
                self.transpose_to(
                    sg, 0, nch, self.wbf[1],
                    lambda cg, n, q=q: self.wbf[1][
                        :, (nch * q + cg) * 128:
                        (nch * q + cg + n) * 128].rearrange(
                        "p (k j) -> p k j", j=128))
            lgbank = self.pb[2]

            def lgmm(te, t=t, lgbank=lgbank):
                last = None
                for c in range(FC):
                    last = te.matmul(
                        out=lgbank[0:24, 0:128],
                        lhsT=self.wbf[0][:, c * 24:(c + 1) * 24],
                        rhs=self.wbf[1][:, c * 128:(c + 1) * 128],
                        start=(c == 0), stop=(c == FC - 1))
                return last
            p.step("tensor", lgmm, reads=[self.wbf[1], self.wbf[0]],
                   writes=[lgbank])
            p.step("scalar", lambda s, lgbank=lgbank: s.activation(
                out=self.lgt[0:24, 0:128],
                in_=lgbank[0:24, 0:128], func=AF.Identity,
                bias=self.sbases[:, 1:2],
                scale=self.sc[0:24, CN["tau"]:CN["tau"] + 1]),
                reads=[lgbank, self.sbases, self.sc], writes=[self.lgt])
            p.step("tensor", lambda te: te.transpose(
                out=self.pb[6][:, 0:24],
                in_=self.lgt[0:24, 0:128],
                identity=self.sid_f[0:24, 0:24]),
                reads=[self.lgt, self.sid_f], writes=[self.pb[6]])
            p.step("vector", lambda v, t=t: v.tensor_copy(
                out=self.Ltm[:, t * 24:(t + 1) * 24],
                in_=self.pb[6][:, 0:24]),
                reads=[self.pb[6]], writes=[self.Ltm])
        Lv = self.Ltm[:].rearrange("p (t k) -> p t k", k=24)
        for off in (16, 20):
            sl = Lv[:, :, off:off + 4]
            m1 = self.sinkM[:, 0:T].rearrange("p (t o) -> p t o", o=1)
            p.step("vector", lambda v, sl=sl, m1=m1: v.reduce_max(
                out=m1, in_=sl, axis=AX.X),
                reads=[self.Ltm], writes=[self.sinkM])
            mb = m1.broadcast_to([128, T, 4])
            p.step("vector", lambda v, sl=sl, mb=mb: v.tensor_sub(
                out=sl, in0=sl, in1=mb),
                reads=[self.Ltm, self.sinkM], writes=[self.Ltm])
            p.step("scalar", lambda s, sl=sl: s.activation(
                out=sl, in_=sl, func=AF.Exp),
                reads=[self.Ltm], writes=[self.Ltm])
            p.step("vector", lambda v, sl=sl, m1=m1: v.reduce_sum(
                out=m1, in_=sl, axis=AX.X),
                reads=[self.Ltm], writes=[self.sinkM])
            p.step("vector", lambda v: v.reciprocal(
                out=self.sinkM[:, T:2 * T], in_=self.sinkM[:, 0:T]),
                reads=[self.sinkM], writes=[self.sinkM])
            rb = self.sinkM[:, T:2 * T].rearrange(
                "p (t o) -> p t o", o=1).broadcast_to([128, T, 4])
            p.step("vector", lambda v, sl=sl, rb=rb: v.tensor_mul(
                out=sl, in0=sl, in1=rb),
                reads=[self.Ltm, self.sinkM], writes=[self.Ltm])
        for t in range(T):
            xs = self.load_xtile(t)
            for n in range(cfg.NS):
                sg = self.stg[n % 2]
                self.norm_slice(t, xs, n * C, (n + 1) * C, sg[:, 0:C], sg,
                                eng=("vector", "scalar")[n % 2], plain=True)
                hc = self.Ltm[:, t * 24 + 16 + n: t * 24 + 17 + n]
                if n == 0:
                    p.step("scalar", lambda s, sg=sg, hc=hc: s.activation(
                        out=self.xpb[:, 0:C], in_=sg[:, 0:C],
                        func=AF.Identity, scale=hc),
                        reads=[sg, self.Ltm], writes=[self.xpb])
                elif n == 2:
                    tb = self.tbx[1]
                    p.step("scalar", lambda s, sg=sg, hc=hc, tb=tb:
                           s.activation(out=tb[:, 0:C], in_=sg[:, 0:C],
                                        func=AF.Identity, scale=hc),
                           reads=[sg, self.Ltm], writes=[tb])
                    p.step("vector", lambda v, tb=tb: v.tensor_add(
                        out=self.xpb[:, 0:C], in0=self.xpb[:, 0:C],
                        in1=tb[:, 0:C]),
                        reads=[tb, self.xpb], writes=[self.xpb])
                else:
                    p.step("vector", lambda v, sg=sg, hc=hc:
                           v.scalar_tensor_tensor(
                               out=self.xpb[:, 0:C], in0=sg[:, 0:C],
                               scalar=hc, in1=self.xpb[:, 0:C],
                               op0=ALU.mult, op1=ALU.add),
                           reads=[sg, self.Ltm, self.xpb], writes=[self.xpb])
            self.transpose_to(
                self.xpb, 0, FCC, self.xpT,
                lambda cg, n8, t=t: self.xpT[:].rearrange(
                    "p (c n) -> p c n", n=NT)[:, cg:cg + n8,
                                              t * 128:(t + 1) * 128])
        def evy(s, bA, bB):
            stgy = self.stg[2]
            for pan in range(cfg.NPAN):
                p0, p1 = pan * 512, min(NT, (pan + 1) * 512)
                bank = (bA, bB)[pan]
                p.step("scalar", lambda sc, bank=bank, p0=p0, p1=p1:
                       sc.copy(out=stgy[:, p0:p1], in_=bank[:, 0:p1 - p0]),
                       reads=[bank], writes=[stgy])
            self.v_transpose_store(stgy, s, self.d["y0"])
        self.emit_wproj("ybr", self.i["w_br0"], FCC, evy,
                        moving_ap_fn=lambda c, p0, p1: self.xpT[
                            :, c * NT + p0: c * NT + p1],
                        moving_bufs=[self.xpT], fc=FCC)
        halves = []
        T2 = (T + 1) // 2
        for hx, (t0, t1) in enumerate([(0, T2), (T2, T)]):
            if t0 >= t1:
                continue
            nt = t1 - t0
            bE, bP = self.sinkEh[hx], self.sinkPh[hx]
            bU, bV, bM = self.sinkUh[hx], self.sinkVh[hx], self.sinkMh[hx]
            Zh = Lv[:, t0:t1, 0:16].rearrange("p t (i j) -> p t i j", j=4)
            E4 = bE[:, 0:nt * 16].rearrange("p (t i j) -> p t i j", i=4, j=4)
            P4 = bP[:, 0:nt * 16].rearrange("p (t i j) -> p t i j", i=4, j=4)
            P4T = bP[:, 0:nt * 16].rearrange("p (t i j) -> p t j i", i=4, j=4)
            u2 = bU[:, 0:nt * 4].rearrange("p (t i) -> p t i", i=4)
            v2 = bV[:, 0:nt * 4].rearrange("p (t j) -> p t j", j=4)
            sr = bM[:, 0:nt * 4].rearrange("p (t i) -> p t i", i=4)
            sc2 = bM[:, 4 * T2:4 * T2 + nt * 4].rearrange(
                "p (t j) -> p t j", j=4)
            cb = v2.unsqueeze(2).broadcast_to([128, nt, 4, 4])
            rb = u2.unsqueeze(3).broadcast_to([128, nt, 4, 4])
            sr0 = sr[:, :, 0:1].broadcast_to([128, nt, 4])
            sc0 = sc2[:, :, 0:1].broadcast_to([128, nt, 4])
            mb0 = sr.unsqueeze(3).broadcast_to([128, nt, 4, 4])
            ops = []

            def add(eng, fn, reads, writes, ops=ops):
                ops.append((eng, fn, reads, writes))

            add("vector", lambda v, Zh=Zh, sr=sr: v.reduce_max(
                out=sr, in_=Zh, axis=AX.X), [self.Ltm], [bM])
            add("vector", lambda v, Zh=Zh, E4=E4, mb0=mb0: v.tensor_sub(
                out=E4, in0=Zh, in1=mb0), [self.Ltm, bM], [bE])
            add("scalar", lambda s, bE=bE, nt=nt: s.activation(
                out=bE[:, 0:nt * 16], in_=bE[:, 0:nt * 16], func=AF.Exp),
                [bE], [bE])
            add("vector", lambda v, bV=bV, nt=nt: v.memset(
                bV[:, 0:nt * 4], 1.0), [], [bV])
            for it in range(cfg.sink_iters):
                last = it == cfg.sink_iters - 1
                resc = it < 4 or it % 4 == 0
                add("vector", lambda v, P4=P4, E4=E4, cb=cb: v.tensor_mul(
                    out=P4, in0=E4, in1=cb), [bE, bV], [bP])
                add("vector", lambda v, sr=sr, P4=P4: v.reduce_sum(
                    out=sr, in_=P4, axis=AX.X), [bP], [bM])
                eng_r = "vector"
                add(eng_r, lambda v, bU=bU, bM=bM, nt=nt: v.reciprocal(
                    out=bU[:, 0:nt * 4], in_=bM[:, 0:nt * 4]), [bM], [bU])
                if resc:
                    add("vector", lambda v, u2=u2, sr0=sr0: v.tensor_mul(
                        out=u2, in0=u2, in1=sr0), [bU, bM], [bU])
                add("vector", lambda v, P4=P4, E4=E4, rb=rb: v.tensor_mul(
                    out=P4, in0=E4, in1=rb), [bE, bU], [bP])
                add("vector", lambda v, sc2=sc2, P4T=P4T: v.reduce_sum(
                    out=sc2, in_=P4T, axis=AX.X), [bP], [bM])
                add(eng_r, lambda v, bV=bV, bM=bM, nt=nt: v.reciprocal(
                    out=bV[:, 0:nt * 4],
                    in_=bM[:, 4 * T2:4 * T2 + nt * 4]), [bM], [bV])
                if not last and resc:
                    add("vector", lambda v, v2=v2, sc0=sc0: v.tensor_mul(
                        out=v2, in0=v2, in1=sc0), [bV, bM], [bV])
            add("vector", lambda v, P4=P4, E4=E4, rb=rb: v.tensor_mul(
                out=P4, in0=E4, in1=rb), [bE, bU], [bP])
            add("vector", lambda v, E4=E4, P4=P4, cb=cb: v.tensor_mul(
                out=E4, in0=P4, in1=cb), [bP, bV], [bE])
            halves.append(ops)
        for k in range(max(len(o) for o in halves)):
            for ops in halves:
                if k < len(ops):
                    eng, fn, rd, wr = ops[k]
                    p.step(eng, fn, reads=rd, writes=wr)
        p.step("tensor", lambda te: te.matmul(
            out=self.pb[6][:, 0:1], lhsT=self.sonesr[:],
            rhs=self.sgate[0:1, 1:2], start=True, stop=True),
            reads=[self.sgate, self.sonesr], writes=[self.pb[6]])
        p.step("vector", lambda v: v.tensor_copy(out=st[:, 4 * T:4 * T + 1],
                                                 in_=self.pb[6][:, 0:1]),
               reads=[self.pb[6]], writes=[st])
        p.step("vector", lambda v: v.tensor_scalar(
            out=st[:, 4 * T + 1:4 * T + 2], in0=st[:, 4 * T:4 * T + 1],
            scalar1=self.ccol("negone"),
            scalar2=self.ccol("one"), op0=ALU.mult, op1=ALU.add),
            reads=[st, self.sc], writes=[st])
        T2 = (T + 1) // 2
        for hx, (t0, t1) in enumerate([(0, T2), (T2, T)]):
            if t0 >= t1:
                continue
            nt = t1 - t0
            bE = self.sinkEh[hx]
            p.step("vector", lambda v, t0=t0, t1=t1, bE=bE, nt=nt:
                   v.tensor_scalar(
                       out=Lv[:, t0:t1, 0:16],
                       in0=bE[:, 0:nt * 16].rearrange("p (t k) -> p t k", k=16),
                       scalar1=st[:, 4 * T:4 * T + 1], scalar2=None, op0=ALU.mult),
                   reads=[bE, st], writes=[self.Ltm])
        diag = Lv[:, :, 0:16:5]
        p.step("vector", lambda v, diag=diag: v.tensor_scalar(
            out=diag, in0=diag, scalar1=st[:, 4 * T + 1:4 * T + 2],
            scalar2=None,
            op0=ALU.add),
            reads=[self.Ltm, st], writes=[self.Ltm])
        if self.preload_next is not None:
            self.preload_w(*self.preload_next)
            self.preload_next = None
        for t in range(T):
            xs = self.load_xtile(t)
            for jS in range(cfg.NS):
                self.norm_slice(t, xs, jS * C, (jS + 1) * C,
                                self.stg[jS][:, 0:C], self.stg[jS],
                                eng=("vector", "scalar")[jS % 2], plain=True)
            yt_b = self.ytile if t % 2 == 0 else self.lrows
            yt_ap = (self.ytile[:, 0:C] if t % 2 == 0
                     else self.lrows.h[:].bitcast(BF16)[:, 0:C])
            p.dma_step("scalar", None, yt_b,
                       yt_ap.rearrange("p (s j) -> p s j", j=128),
                       self.d["y0"],
                       self.d["y0"][:].rearrange(
                           "(s t p) j -> p s t j", p=128, t=T)[:, :, t, :])
            accs = [self.accb[0], self.accb[1]]
            engs = ["vector", "vector"]
            for iS0 in range(0, cfg.NS, 2):
                pair = [iS0, iS0 + 1]
                for jS in range(cfg.NS):
                    for k2, iS in enumerate(pair):
                        acc = accs[k2]
                        cc = self.Ltm[:, t * 24 + 4 * iS + jS:
                                      t * 24 + 4 * iS + jS + 1]
                        sg = self.stg[jS]
                        if jS == 0:
                            p.step("scalar", lambda s, sg=sg, cc=cc, acc=acc:
                                   s.activation(out=acc[:, 0:C],
                                                in_=sg[:, 0:C],
                                                func=AF.Identity, scale=cc),
                                   reads=[sg, self.Ltm], writes=[acc])
                        elif k2 == 0:
                            p.step("vector", lambda v, sg=sg, cc=cc, acc=acc:
                                   v.scalar_tensor_tensor(
                                       out=acc[:, 0:C], in0=sg[:, 0:C],
                                       scalar=cc, in1=acc[:, 0:C],
                                       op0=ALU.mult, op1=ALU.add),
                                   reads=[sg, self.Ltm, acc], writes=[acc])
                        else:
                            tb = self.tbx[jS % 2]
                            p.step("scalar", lambda s, sg=sg, cc=cc, tb=tb:
                                   s.activation(out=tb[:, 0:C],
                                                in_=sg[:, 0:C],
                                                func=AF.Identity, scale=cc),
                                   reads=[sg, self.Ltm], writes=[tb])
                            p.step("vector", lambda v, tb=tb, acc=acc:
                                   v.tensor_add(out=acc[:, 0:C],
                                                in0=acc[:, 0:C],
                                                in1=tb[:, 0:C]),
                                   reads=[tb, acc], writes=[acc])
                for k2, iS in enumerate(pair):
                    acc = accs[k2]
                    hp = self.Ltm[:, t * 24 + 20 + iS: t * 24 + 21 + iS]
                    if k2 == 0:
                        p.step("vector", lambda v, hp=hp, acc=acc,
                               yt_ap=yt_ap:
                               v.scalar_tensor_tensor(
                                   out=acc[:, 0:C], in0=yt_ap,
                                   scalar=hp, in1=acc[:, 0:C],
                                   op0=ALU.mult, op1=ALU.add),
                               reads=[yt_b, self.Ltm, acc],
                               writes=[acc])
                    else:
                        tb = self.tbx[1]
                        p.step("scalar", lambda s, hp=hp, tb=tb, yt_ap=yt_ap:
                               s.activation(
                                   out=tb[:, 0:C], in_=yt_ap,
                                   func=AF.Identity, scale=hp),
                               reads=[yt_b, self.Ltm], writes=[tb])
                        p.step("vector", lambda v, tb=tb, acc=acc:
                               v.tensor_add(out=acc[:, 0:C], in0=acc[:, 0:C],
                                            in1=tb[:, 0:C]),
                               reads=[tb, acc], writes=[acc])
                for k2, iS in enumerate(pair):
                    acc = accs[k2]
                    p.step("vector", lambda v, iS=iS, acc=acc, xs=xs:
                           v.tensor_add(
                               out=xs[:, iS * C:(iS + 1) * C],
                               in0=xs[:, iS * C:(iS + 1) * C],
                               in1=acc[:, 0:C]),
                           reads=[acc] + self.xr(t, iS * C, (iS + 1) * C),
                           writes=self.xr(t, iS * C, (iS + 1) * C))
            W2 = self.SLW
            for q in range(self.NXCH):
                p.dma_step("scalar", f"xwb{q}", self.d["Xc"][t][q],
                           self.Xap[t * 128:(t + 1) * 128,
                                    q * W2:(q + 1) * W2],
                           self.xr(t, q * W2, (q + 1) * W2),
                           xs[:, q * W2:(q + 1) * W2])
            if fuse_next_ln:
                cands = [self.qk_tm, self.h_fm, self.hp, self.xpb]
                lnstg = [b for b in cands if b.h.shape[1] >= self.SLW]
                while len(lnstg) < 4:
                    lnstg.append(self.stg[len(lnstg)])
                real_stats = self.stats
                self.stats = self.stats2
                self.emit_ln_tile(t, xs, stgs=lnstg)
                self.stats = real_stats

    def emit_l1ffn(self):
        cfg, p = self.cfg, self.p
        T, NT, FC = cfg.T, cfg.NT, cfg.FC
        st = self.stats
        self.emit_ln_to_actT()
        if not self._y_seeded:
            for t in range(T):
                tok = p.dma_step(
                    "sync", f"yt{t % 4}", self.i["Yt"][t],
                    self.Yap[t * 128:(t + 1) * 128, :],
                    self.d["Xc"][t], self.Xap[t * 128:(t + 1) * 128, :])
                self.final_deps.append(tok)
        for c in range(FC):
            stgb = self.wstg[c % 4]
            p.dma_step("sync", None, stgb, stgb[:, 0:512],
                       self.i["w_qk1"], self.i["w_qk1"][0, :, c * 512:(c + 1) * 512])
            wqb = self.wbf[c % 2]
            p.step("scalar", lambda sc, stgb=stgb, wqb=wqb: sc.copy(
                out=wqb[:, 0:512], in_=stgb[:, 0:512]),
                reads=[stgb], writes=[wqb])

            def qkmm(te, c=c, wqb=wqb):
                last = None
                for t in range(T):
                    last = te.matmul(
                        out=self.pb[t][:, 0:512],
                        lhsT=self.actT[:, c * NT + t * 128: c * NT + (t + 1) * 128],
                        rhs=wqb[:, 0:512], start=(c == 0), stop=(c == FC - 1))
                return last
            p.step("tensor", qkmm, reads=[wqb, self.actT],
                   writes=list(self.pb[0:T]))
        for t in range(T):
            p.step("scalar", lambda s, t=t: s.copy(
                out=self.qk_tm[:, (t % 2) * 512:(t % 2) * 512 + 512],
                in_=self.pb[t][:, 0:512]),
                reads=[self.pb[t]], writes=[self.qk_tm])
            q = self.qk_tm[:, (t % 2) * 512: (t % 2) * 512 + 256]
            k = self.qk_tm[:, (t % 2) * 512 + 256: (t % 2) * 512 + 512]
            p.step("vector", lambda v, q=q, k=k: v.tensor_mul(
                out=self.lrows[:, 0:256], in0=q, in1=k),
                reads=[self.qk_tm], writes=[self.lrows])
            sl = self.w_tm[:, t * 8:(t + 1) * 8]
            p.step("vector", lambda v, t=t: v.reduce_sum(
                out=self.w_tm[:, t * 8:(t + 1) * 8].rearrange(
                    "p (e o) -> p e o", o=1),
                in_=self.lrows[:, 0:256].rearrange("p (e d) -> p e d", d=cfg.DK),
                axis=AX.X),
                reads=[self.lrows], writes=[self.w_tm])
            p.step("vector", lambda v, sl=sl: v.tensor_scalar(
                out=sl, in0=sl, scalar1=self.ccol("scaleDK"), scalar2=None,
                op0=ALU.mult), reads=[self.w_tm, self.sc], writes=[self.w_tm])
            p.step("vector", lambda v, sl=sl: v.reduce_max(
                out=st[:, 4 * T:4 * T + 1], in_=sl, axis=AX.X),
                reads=[self.w_tm], writes=[st])
            p.step("vector", lambda v, sl=sl: v.tensor_scalar(
                out=sl, in0=sl, scalar1=st[:, 4 * T:4 * T + 1], scalar2=None,
                op0=ALU.subtract), reads=[self.w_tm, st], writes=[self.w_tm])
            p.step("scalar", lambda s, sl=sl: s.activation(
                out=sl, in_=sl, func=AF.Exp),
                reads=[self.w_tm], writes=[self.w_tm])
            p.step("vector", lambda v, sl=sl: v.reduce_sum(
                out=st[:, 4 * T:4 * T + 1], in_=sl, axis=AX.X),
                reads=[self.w_tm], writes=[st])
            p.step("vector", lambda v: v.reciprocal(out=st[:, 4 * T + 1:4 * T + 2],
                                                    in_=st[:, 4 * T:4 * T + 1]),
                   reads=[st], writes=[st])
            p.step("vector", lambda v, sl=sl: v.tensor_scalar(
                out=sl, in0=sl, scalar1=st[:, 4 * T + 1:4 * T + 2], scalar2=None,
                op0=ALU.mult), reads=[self.w_tm, st], writes=[self.w_tm])
        def evA(s, bA, bB):
            for pan in range(cfg.NPAN):
                p0, p1 = pan * 512, min(NT, (pan + 1) * 512)
                bank = (bA, bB)[pan]
                p.step("scalar", lambda sc, bank=bank, p0=p0, p1=p1:
                       sc.copy(out=self.h_fm[:, p0:p1], in_=bank[:, 0:p1 - p0]),
                       reads=[bank], writes=[self.h_fm])
        self.emit_wproj("A1", self.i["w_A1"], 1, evA)
        for t in range(T):
            p.step("tensor", lambda te, t=t: te.transpose(
                out=self.pb[6][0:8, 0:128],
                in_=self.w_tm[:, t * 8:(t + 1) * 8], identity=self.sid_f[:]),
                reads=[self.w_tm, self.sid_f], writes=[self.pb[6]])
            p.step("vector", lambda v, t=t: v.tensor_copy(
                out=self.wT[0:8, t * 128:(t + 1) * 128],
                in_=self.pb[6][0:8, 0:128]),
                reads=[self.pb[6]], writes=[self.wT])
        for pan in range(cfg.NPAN):
            p0, p1 = pan * 512, min(NT, (pan + 1) * 512)
            bank = self.pb[4 + pan]
            p.step("tensor", lambda te, bank=bank, p0=p0, p1=p1: te.matmul(
                out=bank[:, 0:p1 - p0], lhsT=self.ssel[0:cfg.E, :],
                rhs=self.wT[0:cfg.E, p0:p1], start=True, stop=True),
                reads=[self.ssel, self.wT], writes=[bank])
            p.step("vector", lambda v, bank=bank, p0=p0, p1=p1: v.tensor_mul(
                out=self.hp[:, p0:p1], in0=self.h_fm[:, p0:p1],
                in1=bank[:, 0:p1 - p0]),
                reads=[self.h_fm, bank], writes=[self.hp])
        def pre_slice(s):
            stgb = self.bmstg
            bmt = self.bmt[s % 2]
            p.dma_step("sync", None, stgb, stgb[:, 0:128],
                       self.i["w_Bm1"], self.i["w_Bm1"][s, :, :])
            p.step("vector", lambda v, stgb=stgb, bmt=bmt: v.tensor_copy(
                out=bmt[:], in_=stgb[:, 0:128]),
                reads=[stgb], writes=[bmt])

        def extra_mm(te, s, bA, bB):
            last = None
            for pan in range(cfg.NPAN):
                p0, p1 = pan * 512, min(NT, (pan + 1) * 512)
                bank = (bA, bB)[pan]
                last = te.matmul(out=bank[:, 0:p1 - p0],
                                 lhsT=self.bmt[s % 2][:],
                                 rhs=self.hp[:, p0:p1], start=False, stop=True)
            return last
        self.emit_outproj("W1", self.i["w_W1"], self.i["b_W1"], dst="y",
                          extra_mm=extra_mm,
                          extra_reads=[self.bmt[0], self.bmt[1], self.hp],
                          pre_slice=pre_slice)

    def build(self):
        cfg = self.cfg
        self.declare()
        self.load_inputs()
        for blk in cfg.blocks:
            if blk == "l0a":
                first = blk == cfg.blocks[0]
                self.preload_w(self.i["w_qkv0"], 2 * cfg.FC)
                self.emit_ln_to_actT(src=self.i["x"] if first else None)
                if first:
                    for t in range(cfg.T):
                        self.p.dma_step(
                            "scalar", f"xin{t}", self.d["Xc"][t],
                            self.Xap[t * 128:(t + 1) * 128, :],
                            self.i["x"], self.i["x"][t * 128:(t + 1) * 128, :])
                self.emit_qkv(0)
                self.emit_attention(0)
                self.emit_outproj("o0", self.i["w_o0"], self.i["b_o0"])
            elif blk == "l0f":
                nxt = cfg.blocks[cfg.blocks.index(blk) + 1:]
                fuse = bool(nxt) and nxt[0] == "l1a"
                if fuse:
                    self.preload_next = (self.i["w_q1"], cfg.FC)
                self.emit_l0ffn(fuse_next_ln=fuse)
                self._l1a_ready = fuse
            elif blk == "l1a":
                both = "l1f" in cfg.blocks
                if not getattr(self, "_l1a_ready", False):
                    self.emit_ln_to_actT()
                if both:
                    for t in range(cfg.T):
                        tok = self.p.dma_step(
                            "sync", f"yt{t % 4}", self.i["Yt"][t],
                            self.Yap[t * 128:(t + 1) * 128, :],
                            self.d["Xc"][t],
                            self.Xap[t * 128:(t + 1) * 128, :])
                        self.final_deps.append(tok)
                    self._y_seeded = True
                self.emit_qkv(1)
                self.emit_attention(1)
                self.emit_outproj("o1", self.i["w_o1"], None,
                                  dst="Xy" if both else "X")
            elif blk == "l1f":
                self.emit_l1ffn()
        if "l1f" not in cfg.blocks:
            for t in range(cfg.T):
                tok = self.p.dma_step(
                    "sync", f"yt{t % 4}", self.i["Yt"][t],
                    self.Yap[t * 128:(t + 1) * 128, :],
                    self.d["Xc"][t], self.Xap[t * 128:(t + 1) * 128, :])
                self.final_deps.append(tok)
        return self.p.finalize("sync", self.final_deps)


def prep_inputs(cfg, inputs):
    D, T, FC = cfg.D, cfg.T, cfg.FC
    x = np.asarray(inputs["L0_attn_x"], np.float32)
    B = x.shape[0]
    shared = {}
    shared["w_qkv0"] = tile_w(inputs["L0_attn_attn_qkvw"])
    shared["b_qkv0"] = np.ascontiguousarray(
        np.asarray(inputs["L0_attn_attn_qkvb"], np.float32).reshape(3 * FC, 128).T)
    shared["w_o0"] = tile_w(inputs["L0_attn_attn_ow"])
    shared["b_o0"] = np.ascontiguousarray(
        np.asarray(inputs["L0_attn_attn_ob"], np.float32).reshape(FC, 128).T)
    wres = np.concatenate([np.asarray(inputs["L0_ffn_W_res"], np.float32),
                           np.asarray(inputs["L0_ffn_W_pre"], np.float32),
                           np.asarray(inputs["L0_ffn_W_post"], np.float32)], 0)
    shared["w_res0"] = tile_w(np.ascontiguousarray(wres.T), ow=24)
    shared["bases0"] = np.ascontiguousarray(np.concatenate(
        [np.asarray(inputs["L0_ffn_H_res_base"], np.float32).ravel(),
         np.asarray(inputs["L0_ffn_H_pre_base"], np.float32),
         np.asarray(inputs["L0_ffn_H_post_base"], np.float32)])[:, None])
    shared["gate0"] = np.asarray(inputs["L0_ffn_gate"],
                                 np.float32).reshape(1, 1)
    shared["w_br0"] = tile_w(np.asarray(inputs["L0_ffn_W_branch"], np.float32),
                             fc=cfg.FCC, ow=128)
    shared["w_q1"] = tile_w(np.ascontiguousarray(
        np.asarray(inputs["L1_attn_Wq"], np.float32).T))
    shared["w_k1"] = tile_w(np.ascontiguousarray(
        np.asarray(inputs["L1_attn_Wk"], np.float32).T))
    shared["w_v1"] = tile_w(np.ascontiguousarray(
        np.asarray(inputs["L1_attn_Wv"], np.float32).T))
    shared["w_o1"] = tile_w(np.ascontiguousarray(
        np.asarray(inputs["L1_attn_Wo"], np.float32).T))
    shared["w_W1"] = tile_w(np.ascontiguousarray(
        np.asarray(inputs["L1_ffn_W"], np.float32).T))
    shared["b_W1"] = np.ascontiguousarray(
        np.asarray(inputs["L1_ffn_b"], np.float32).reshape(FC, 128).T)
    wqk = np.concatenate([np.asarray(inputs["L1_ffn_Wq"], np.float32),
                          np.asarray(inputs["L1_ffn_Wk"], np.float32)], 0)
    shared["w_qk1"] = tile_w(np.ascontiguousarray(wqk.T), ow=512)
    shared["w_A1"] = tile_w(np.ascontiguousarray(
        np.asarray(inputs["L1_ffn_A"], np.float32).reshape(cfg.ER, D).T))
    bmpk = np.ascontiguousarray(
        np.asarray(inputs["L1_ffn_Bm"], np.float32).transpose(0, 2, 1).reshape(
            cfg.ER, D))
    shared["w_Bm1"] = tile_w(bmpk, fc=1, ow=128)

    in_maps = []
    for core in range(cfg.n_cores):
        b, par = core // 2, core % 2
        hc = host_constants(cfg, core)
        xb = x[b].reshape(cfg.G, 128, D)
        rows = [2 * t + par for t in range(T)]
        xcore = np.ascontiguousarray(xb[rows].reshape(T * 128, D))
        m = dict(shared)
        m.update(hc)
        m["x"] = xcore
        in_maps.append(m)
    return in_maps


def gather_output(cfg, inputs, results):
    x = np.asarray(inputs["L0_attn_x"], np.float32)
    B, S, D = x.shape
    out = np.empty((B, S, D), np.float32)
    for core in range(cfg.n_cores):
        b, par = core // 2, core % 2
        y = np.asarray(results[core]["y"]).reshape(cfg.T, 128, D)
        for t in range(cfg.T):
            g = 2 * t + par
            out[b, g * 128:(g + 1) * 128, :] = y[t]
    return out


_NC_CACHE = {}


def kernel(**inputs):
    cfg = Cfg()
    key = "full"
    if key not in _NC_CACHE:
        _NC_CACHE[key] = Builder(cfg).build()
    nc = _NC_CACHE[key]
    in_maps = prep_inputs(cfg, inputs)
    res = run_bass_kernel_spmd(nc, in_maps, core_ids=list(range(cfg.n_cores)))
    return gather_output(cfg, inputs, res.results)


# version 30
# speedup vs baseline: 1.0217x; 1.0217x over previous
import math
import numpy as np
import ml_dtypes
import concourse.bass as bass
import concourse.mybir as mybir
from concourse.bass_utils import run_bass_kernel_spmd

BF16 = mybir.dt.bfloat16
F32 = mybir.dt.float32
AF = mybir.ActivationFunctionType
ALU = mybir.AluOpType
AX = mybir.AxisListType
BF = ml_dtypes.bfloat16

ENGINES = ("sync", "tensor", "vector", "scalar", "gpsimd")


class Buf:
    __slots__ = ("h", "last_write", "readers", "name")

    def __init__(self, h, name=""):
        self.h = h
        self.name = name
        self.last_write = None
        self.readers = []

    def __getitem__(self, key):
        return self.h[key]

    def ap(self):
        return self.h[:]


class SubBuf:

    __slots__ = ("par", "off")

    def __init__(self, par, off):
        self.par = par
        self.off = off

    @property
    def h(self):
        return self.par.h

    @property
    def name(self):
        return self.par.name

    @property
    def last_write(self):
        return self.par.last_write

    @last_write.setter
    def last_write(self, v):
        self.par.last_write = v

    @property
    def readers(self):
        return self.par.readers

    @readers.setter
    def readers(self, v):
        self.par.readers = v

    def __getitem__(self, key):
        pk, ck = key
        c2 = slice((ck.start or 0) + self.off, ck.stop + self.off)
        return self.par.h[pk, c2]

    def ap(self):
        return self.par.h[:, self.off:self.off + 512]


class Prog:
    def __init__(self, nc):
        self.nc = nc
        self.steps = {e: [] for e in ENGINES}
        self.tick = {}
        self.tick_n = {}
        for e in ENGINES:
            self.tick[e] = nc.alloc_semaphore(f"tk_{e}")
            self.tick_n[e] = 0
        self.dma_ch = {}
        self.satisfied = {e: {} for e in ENGINES}
        self.n_waits = 0
        self.n_steps = 0

    def buf(self, h, name=""):
        return Buf(h, name)

    def sbuf(self, name, shape, dtype):
        return Buf(self.nc.alloc_sbuf_tensor(name, shape, dtype), name)

    def psum(self, name, shape, dtype=F32):
        return Buf(self.nc.alloc_psum_tensor(name, shape, dtype), name)

    def dram(self, name, shape, dtype, kind="Internal"):
        t = self.nc.dram_tensor(name, shape, dtype, kind=kind)
        return Buf(t.ap(), name)

    def dma_channel(self, name):
        if name not in self.dma_ch:
            self.dma_ch[name] = [self.nc.alloc_semaphore(f"dc_{name}"), 0]
        return name

    def _dep_tokens(self, reads, writes):
        toks = []
        for b in reads:
            if b.last_write is not None:
                toks.append(b.last_write)
        for b in writes:
            if b.last_write is not None:
                toks.append(b.last_write)
            toks.extend(b.readers)
        return toks

    def _emit_waits(self, engine_name, eng, toks, self_tok_limit):
        need = {}
        for sem, val in toks:
            if val > need.get(sem, -1):
                need[sem] = val
        sat = self.satisfied[engine_name]
        for sem, val in sorted(need.items(), key=lambda kv: kv[0].num):
            if sat.get(sem, 0) >= val:
                continue
            if sem is self.tick[engine_name] and val > self_tok_limit:
                raise RuntimeError(
                    f"deadlock: {engine_name} waits own future tick {val}")
            eng.wait_ge(sem, val)
            self.n_waits += 1
            sat[sem] = val

    def step(self, engine, fn, reads=(), writes=(), extra_deps=(), name=""):
        reads = [b for b in reads if b is not None]
        writes = [b for b in writes if b is not None]
        toks = self._dep_tokens(reads, writes) + list(extra_deps)
        sem = self.tick[engine]
        self.tick_n[engine] += 1
        myval = self.tick_n[engine]
        tok = (sem, myval)
        self_limit = myval - 1

        def run(eng, toks=toks, fn=fn):
            self._emit_waits(engine, eng, toks, self_limit)
            last = fn(eng)
            if last is None:
                last = eng.nop()
            last.then_inc(sem, 1)

        self.steps[engine].append(run)
        self.n_steps += 1
        for b in writes:
            b.last_write = tok
            b.readers = []
        for b in reads:
            b.readers.append(tok)
        return tok

    def dma_step(self, engine, channel, out_b, out_ap, in_b, in_ap,
                 extra_deps=(), name="", **dma_kwargs):
        writes = [b for b in (out_b if isinstance(out_b, (list, tuple))
                              else [out_b]) if b is not None]
        reads = [b for b in (in_b if isinstance(in_b, (list, tuple))
                             else [in_b]) if b is not None]
        if channel is None:
            channel = (writes + reads)[0].name
        channel = f"{channel}@{'sw' if engine == 'gpsimd' else 'hw'}"
        self.dma_channel(channel)
        ch = self.dma_ch[channel]
        sem = ch[0]
        prev = ch[1]
        ch[1] += 16
        tok = (sem, ch[1])
        toks = self._dep_tokens(reads, writes) + list(extra_deps)
        if prev > 0:
            toks.append((sem, prev))
        self_limit = self.tick_n[engine]

        def run(eng, toks=toks, out_ap=out_ap, in_ap=in_ap, kw=dma_kwargs):
            self._emit_waits(engine, eng, toks, self_limit)
            eng.dma_start(out=out_ap, in_=in_ap, **kw).then_inc(sem, 16)

        self.steps[engine].append(run)
        self.n_steps += 1
        for b in writes:
            b.last_write = tok
            b.readers = []
        for b in reads:
            b.readers.append(tok)
        return tok

    def finalize(self, final_engine="sync", final_deps=()):
        nc = self.nc
        with nc.Block() as block:
            for ename in ENGINES:
                steps = self.steps[ename]
                if not steps and ename != final_engine:
                    continue

                def body(eng, steps=steps, ename=ename):
                    for s in steps:
                        s(eng)
                    if ename == final_engine:
                        need = {}
                        for sem, val in final_deps:
                            if val > need.get(sem, -1):
                                need[sem] = val
                        sat = self.satisfied[ename]
                        for sem, val in sorted(need.items(),
                                               key=lambda kv: kv[0].num):
                            if sat.get(sem, 0) < val:
                                eng.wait_ge(sem, val)

                getattr(block, ename)(body)
        return nc


class Cfg:
    def __init__(self, D=4096, T=8, n_cores=8, blocks=("l0a", "l0f", "l1a", "l1f"),
                 sink_iters=50):
        self.D = D
        self.T = T
        self.HD = 128
        self.FC = D // 128
        self.H0 = self.FC
        self.H1 = self.FC
        self.HKV1 = max(1, self.FC // 4)
        self.DKV1 = self.HKV1 * 128
        self.GQ = self.H1 // self.HKV1
        self.NS = 4
        self.C = D // 4
        self.FCC = self.C // 128
        self.E, self.DK, self.R = 8, 32, 16
        self.ER = self.E * self.R
        self.n_cores = n_cores
        self.blocks = list(blocks)
        self.C0 = 8.0
        self.C1 = 30.0
        self.sink_iters = sink_iters
        self.G = 2 * T
        self.S = self.G * 128
        self.NT = T * 128
        self.NPAN = (self.NT + 511) // 512


CN = dict(negC0=0, negC1=1, scale0=2, invD=3, eps=4, half15=5,
          neghalf=6, rmseps=7, one=8, scaleDK=9, tau=10, negone=11)
NCONST = 12


def host_constants(cfg, core):
    par = core % 2
    consts = np.zeros((128, NCONST), np.float32)
    consts[:, CN["negC0"]] = -cfg.C0
    consts[:, CN["negC1"]] = -cfg.C1
    consts[:, CN["scale0"]] = 1.0 / math.sqrt(cfg.HD)
    consts[:, CN["invD"]] = 1.0 / cfg.D
    consts[:, CN["eps"]] = 1e-5
    consts[:, CN["half15"]] = 1.5
    consts[:, CN["neghalf"]] = -0.5
    consts[:, CN["rmseps"]] = 1.1920929e-07
    consts[:, CN["one"]] = 1.0
    consts[:, CN["scaleDK"]] = 1.0 / math.sqrt(cfg.DK)
    consts[0:16, CN["tau"]] = 10.0
    consts[16:, CN["tau"]] = 1.0
    consts[:, CN["negone"]] = -1.0
    idb = np.eye(128).astype(BF)
    idf = np.eye(128).astype(np.float32)
    onesb = np.ones((128, 1), BF)
    kk = np.arange(128)[:, None]
    qq = np.arange(128)[None, :]
    tri = (kk <= qq).astype(BF)
    mask_r0 = tri if par == 0 else np.ones((128, 128), BF)
    mask_r1 = np.zeros((128, 128), BF) if par == 0 else tri
    D_ = 128
    inv = 1.0 / (500000.0 ** (np.arange(0, D_, 2, dtype=np.float64) / D_))
    wav = 2 * math.pi / inv
    smooth = np.clip((8192.0 / wav - 1.0) / (4.0 - 1.0), 0.0, 1.0)
    newinv = (1 - smooth) * (inv / 8.0) + smooth * inv
    pos = np.concatenate([(2 * t + par) * 128 + np.arange(128)
                          for t in range(cfg.T)])
    fr = np.outer(pos.astype(np.float64), newinv)
    emb = np.concatenate([fr, fr], -1)
    cos_fm = np.ascontiguousarray(np.cos(emb).T.astype(BF))
    sin_fm = np.ascontiguousarray(np.sin(emb).T.astype(BF))
    ones_row_f = np.ones((1, 128), np.float32)
    prot = np.zeros((128, 128), np.float32)
    for dd in range(64):
        prot[dd + 64, dd] = -1.0
        prot[dd, dd + 64] = 1.0
    prot = prot.astype(BF)
    sel = np.zeros((cfg.E, cfg.ER), np.float32)
    for e in range(cfg.E):
        sel[e, e * cfg.R:(e + 1) * cfg.R] = 1.0
    return dict(consts=consts, idb=idb, idf=idf, onesb=onesb,
                mask_r0=mask_r0, mask_r1=mask_r1, cos_fm=cos_fm, sin_fm=sin_fm,
                sel=sel, ones_row_f=ones_row_f, prot=prot)


def tile_w(W, fc=None, ow=128):
    W = np.asarray(W, np.float32)
    Din, O = W.shape
    fc = fc if fc is not None else Din // 128
    assert Din == fc * 128 and O % ow == 0, (W.shape, fc, ow)
    ns = O // ow
    Wt = W.reshape(fc, 128, ns, ow).transpose(2, 1, 0, 3).reshape(ns, 128, fc * ow)
    return np.ascontiguousarray(Wt)


class Builder:
    def __init__(self, cfg):
        self.cfg = cfg
        self.nc = bass.Bass("TRN2", target_bir_lowering=False)
        self.p = Prog(self.nc)
        self.final_deps = []

    def declare(self):
        cfg, p, nc = self.cfg, self.p, self.nc
        D, T, FC, NT = cfg.D, cfg.T, cfg.FC, cfg.NT

        def din(name, shape, dtype=F32):
            return p.buf(nc.dram_tensor(name, shape, dtype,
                                        kind="ExternalInput").ap(), name)

        i = self.i = {}
        i["x"] = din("x", [NT, D])
        i["consts"] = din("consts", [128, NCONST])
        i["idb"] = din("idb", [128, 128], BF16)
        i["idf"] = din("idf", [128, 128])
        i["onesb"] = din("onesb", [128, 1], BF16)
        i["ones_row_f"] = din("ones_row_f", [1, 128])
        i["prot"] = din("prot", [128, 128], BF16)
        i["mask_r0"] = din("mask_r0", [128, 128], BF16)
        i["mask_r1"] = din("mask_r1", [128, 128], BF16)
        i["cos_fm"] = din("cos_fm", [128, NT], BF16)
        i["sin_fm"] = din("sin_fm", [128, NT], BF16)
        i["sel"] = din("sel", [cfg.E, cfg.ER])
        i["w_qkv0"] = din("w_qkv0", [3 * FC, 128, FC * 128])
        i["b_qkv0"] = din("b_qkv0", [128, 3 * FC])
        i["w_o0"] = din("w_o0", [FC, 128, FC * 128])
        i["b_o0"] = din("b_o0", [128, FC])
        i["w_res0"] = din("w_res0", [1, 128, FC * 24])
        i["bases0"] = din("bases0", [24, 1])
        i["gate0"] = din("gate0", [1, 1])
        i["w_br0"] = din("w_br0", [cfg.FCC, 128, cfg.FCC * 128])
        i["w_q1"] = din("w_q1", [FC, 128, FC * 128])
        i["w_k1"] = din("w_k1", [cfg.HKV1, 128, FC * 128])
        i["w_v1"] = din("w_v1", [cfg.HKV1, 128, FC * 128])
        i["w_o1"] = din("w_o1", [FC, 128, FC * 128])
        i["w_W1"] = din("w_W1", [FC, 128, FC * 128])
        i["b_W1"] = din("b_W1", [128, FC])
        i["w_qk1"] = din("w_qk1", [1, 128, FC * 512])
        i["w_A1"] = din("w_A1", [1, 128, FC * 128])
        i["w_Bm1"] = din("w_Bm1", [FC, 128, 128])
        self.SLW = min(1024, D)
        self.NXCH = max(1, D // self.SLW)
        self.Yap = nc.dram_tensor("y", [NT, D], F32,
                                  kind="ExternalOutput").ap()
        i["Yt"] = [p.buf(self.Yap[t * 128:(t + 1) * 128, :], f"yt{t}")
                   for t in range(T)]

        d = self.d = {}
        d["q0"] = p.dram("dq0", [D, NT], BF16)
        d["k0own"] = p.dram("dk0own", [D, NT], BF16)
        d["k0full"] = p.dram("dk0full", [2 * D, NT], BF16)
        d["v0own"] = p.dram("dv0own", [FC * NT, 128], BF16)
        d["v0full"] = p.dram("dv0full", [2 * FC * NT, 128], BF16)
        d["q1"] = p.dram("dq1", [D, NT], BF16)
        d["k1own"] = p.dram("dk1own", [cfg.DKV1, NT], BF16)
        d["k1full"] = p.dram("dk1full", [2 * cfg.DKV1, NT], BF16)
        d["v1own"] = p.dram("dv1own", [cfg.HKV1 * NT, 128], BF16)
        d["v1full"] = p.dram("dv1full", [2 * cfg.HKV1 * NT, 128], BF16)
        for nm, n_h, ro, rf in [("k0", FC, 128, 256), ("v0", FC, NT, 2 * NT),
                                ("k1", cfg.HKV1, 128, 256),
                                ("v1", cfg.HKV1, NT, 2 * NT)]:
            d[nm + "ownH"] = [p.buf(d[nm + "own"].h, f"{nm}ow{h}")
                              for h in range(n_h)]
            d[nm + "fullH"] = [p.buf(d[nm + "full"].h, f"{nm}fl{h}")
                               for h in range(n_h)]
        d["lpk"] = p.dram("dlpk", [1, FC * NT], F32)
        d["lpk2"] = p.dram("dlpk2", [1, FC * NT], F32)

        W = self.SLW
        self.Xap = nc.dram_tensor("dX", [NT, D], F32).ap()
        d["Xc"] = [[p.buf(self.Xap, f"dX{t}c{q}") for q in range(self.NXCH)]
                   for t in range(T)]
        d["y0"] = p.dram("dy0", [cfg.FCC * NT, 128], BF16)
        self.actT = p.sbuf("actT", [128, FC * NT], BF16)
        self.wstg = [p.sbuf(f"wstg{j}", [128, max((FC // 8) * 128, 512)], F32)
                     for j in range(4)]
        self.wbf = [p.sbuf(f"wbf{j}", [128, FC * 128], BF16) for j in range(2)]
        self.stg = [p.sbuf(f"stg{j}", [128, max(NT, 1024)], BF16) for j in range(4)]
        self.xstage = [p.sbuf(f"xstg{j}", [128, D], F32) for j in range(2)]
        self.xsc = [[p.buf(self.xstage[j].h, f"xsc{j}q{q}")
                     for q in range(self.NXCH)] for j in range(2)]
        self.stgF = p.sbuf("stgF", [128, max(NT, 1024)], F32)
        self.accb = [p.sbuf(f"accb{j}", [128, cfg.C], BF16) for j in range(2)]
        self.ST_FS = 4 * T
        self.ST_RMS = 6 * T
        self.stats = p.sbuf("stats", [128, 11 * T], F32)
        self.stsc = [p.sbuf(f"stsc{j}", [128, 2 * self.NXCH], F32)
                     for j in range(2)]
        self.sc = p.sbuf("sc", [128, NCONST], F32)
        self.sid_b = p.sbuf("sid_b", [128, 128], BF16)
        self.sid_f = p.sbuf("sid_f", [128, 128], F32)
        self.sones = p.sbuf("sones", [128, 1], BF16)
        self.sonesr = p.sbuf("sonesr", [1, 128], F32)
        self.sprot = p.sbuf("sprot", [128, 128], BF16)
        self.smask0 = p.sbuf("smask0", [128, 128], BF16)
        self.smask1 = p.sbuf("smask1", [128, 128], BF16)
        self.scos = p.sbuf("scos", [128, NT], BF16)
        self.ssin = p.sbuf("ssin", [128, NT], BF16)

        self.ssel = p.sbuf("ssel", [cfg.E, cfg.ER], F32)
        self.sbias = p.sbuf("sbias", [128, 3 * FC], F32)
        self.sbases = p.sbuf("sbases", [24, 2], F32)
        self.sgate = p.sbuf("sgate", [1, 2], F32)
        self.aK = [p.sbuf(f"aK{j}", [128, 2 * NT], BF16) for j in range(2)]
        self.aV = [p.sbuf(f"aV{j}", [128, 2 * NT], BF16) for j in range(2)]
        self.aQ = [p.sbuf(f"aQ{j}", [128, NT], BF16) for j in range(2)]
        self.aE = [p.sbuf(f"aE{j}", [128, NT], BF16) for j in range(2)]
        self.lrows = p.sbuf("lrows", [128, max(NT, 1024)], F32)
        self.xpb = p.sbuf("xpb", [128, cfg.C], BF16)
        self.ytile = p.sbuf("ytile", [128, cfg.C], BF16)
        self.xpT = p.sbuf("xpT", [128, cfg.FCC * NT], BF16)
        self.lgt = p.sbuf("lgt", [24, 128], F32)
        self.Ltm = p.sbuf("Ltm", [128, T * 24], F32)
        TH = (T + 1) // 2
        self.sinkEh = [p.sbuf(f"sinkE{j}", [128, TH * 16], F32) for j in range(2)]
        self.sinkPh = [p.sbuf(f"sinkP{j}", [128, TH * 16], F32) for j in range(2)]
        self.sinkUh = [p.sbuf(f"sinkU{j}", [128, TH * 4], F32) for j in range(2)]
        self.sinkVh = [p.sbuf(f"sinkV{j}", [128, TH * 4], F32) for j in range(2)]
        self.sinkMh = [p.sbuf(f"sinkM{j}", [128, TH * 8], F32) for j in range(2)]
        self.sinkM = self.sinkMh[0]
        self.sinkU = self.sinkUh[0]
        self.sinkV = self.sinkVh[0]

        self.qk_tm = p.sbuf("qk_tm", [128, 2 * 512], BF16)
        self.w_tm = p.sbuf("w_tm", [128, T * 8], F32)
        self.wT = p.sbuf("wT", [8, NT], F32)
        self.h_fm = p.sbuf("h_fm", [128, NT], BF16)
        self.hp = p.sbuf("hp", [128, NT], BF16)
        self.bmt = [p.sbuf(f"bmt{j}", [128, 128], BF16) for j in range(2)]
        self.bmstg = p.sbuf("bmstg", [128, 128], F32)
        self.pq = [p.psum(f"pq{j}", [128, 1024], F32) for j in range(4)]
        self.pb = [SubBuf(self.pq[j // 2], (j % 2) * 512) for j in range(8)]

    def pbb(self, bank):
        v = self.pq[bank // 2].h[:].bitcast(BF16)
        return v[:, (bank % 2) * 1024:(bank % 2) * 1024 + 1024]

    def ccol(self, name):
        j = CN[name]
        return self.sc[:, j:j + 1]

    def load_inputs(self):
        p, cfg = self.p, self.cfg
        for j, (key, sb) in enumerate([
                ("consts", self.sc), ("idb", self.sid_b),
                ("idf", self.sid_f), ("onesb", self.sones),
                ("mask_r0", self.smask0), ("mask_r1", self.smask1),
                ("cos_fm", self.scos), ("sin_fm", self.ssin),
                ("sel", self.ssel),
                ("ones_row_f", self.sonesr), ("prot", self.sprot)]):
            p.dma_step("sync", f"ini{j % 4}", sb, sb.ap(), self.i[key],
                       self.i[key].ap())
        p.dma_step("sync", "ini3", self.sbases, self.sbases[:, 0:1],
                   self.i["bases0"], self.i["bases0"].ap())
        p.dma_step("sync", "ini2", self.sgate, self.sgate[:, 0:1],
                   self.i["gate0"], self.i["gate0"].ap())
        pass

    def load_xtile(self, t, src=None):
        xs = self.xstage[t % 2]
        W = self.SLW
        for q in range(self.NXCH):
            if src is None:
                inb = self.d["Xc"][t][q]
                inap = self.Xap[t * 128:(t + 1) * 128, q * W:(q + 1) * W]
            else:
                inb = src
                inap = src[t * 128:(t + 1) * 128, q * W:(q + 1) * W]
            self.p.dma_step("sync", f"xld{t % 2}q{q}", self.xsc[t % 2][q],
                            xs[:, q * W:(q + 1) * W], inb, inap)
        return xs

    def xr(self, t, lo, hi):
        W = self.SLW
        return [self.xsc[t % 2][q] for q in range(lo // W, (hi - 1) // W + 1)]

    def emit_tile_sums(self, t, xs):
        cfg, p = self.cfg, self.p
        T, W = cfg.T, self.SLW
        st = self.stats
        nsl = self.NXCH
        ss = self.stsc[t % 2]
        for q in range(nsl):
            xb = self.xsc[t % 2][q]
            if q % 2 == 0:
                p.step("vector", lambda v, q=q, ss=ss: v.reduce_sum(
                    out=ss[:, q:q + 1], in_=xs[:, q * W:(q + 1) * W],
                    axis=AX.X), reads=[xb], writes=[ss])
            else:
                p.step("scalar", lambda s, q=q, ss=ss: s.activation(
                    out=self.stg[3][:, 0:W], in_=xs[:, q * W:(q + 1) * W],
                    func=AF.Identity, accum_out=ss[:, q:q + 1]),
                    reads=[xb], writes=[ss, self.stg[3]])
            p.step("scalar", lambda s, q=q, ss=ss: s.activation(
                out=self.stg[3][:, 0:W], in_=xs[:, q * W:(q + 1) * W],
                func=AF.Square, accum_out=ss[:, nsl + q:nsl + q + 1]),
                reads=[xb], writes=[ss, self.stg[3]])
        p.step("vector", lambda v, ss=ss: v.reduce_sum(
            out=st[:, t:t + 1], in_=ss[:, 0:nsl], axis=AX.X),
            reads=[ss], writes=[st])
        p.step("vector", lambda v, ss=ss: v.reduce_sum(
            out=st[:, T + t:T + t + 1], in_=ss[:, nsl:2 * nsl], axis=AX.X),
            reads=[ss], writes=[st])

    def emit_tile_finish(self, t, rms=False):
        cfg, p = self.cfg, self.p
        T = cfg.T
        st = self.stats
        FS = self.ST_FS
        mu = st[:, 2 * T + t:2 * T + t + 1]
        r = st[:, 3 * T + t:3 * T + t + 1]
        s1 = st[:, FS:FS + 1]
        s2 = st[:, FS + 1:FS + 2]
        var = st[:, FS + 2:FS + 3]
        p.step("vector", lambda v: v.tensor_scalar(
            out=mu, in0=st[:, t:t + 1], scalar1=self.ccol("invD"),
            scalar2=None, op0=ALU.mult), reads=[st, self.sc], writes=[st])
        p.step("vector", lambda v: v.tensor_mul(out=s1, in0=mu, in1=mu),
               reads=[st], writes=[st])
        p.step("vector", lambda v: v.scalar_tensor_tensor(
            out=var, in0=st[:, T + t:T + t + 1], scalar=self.ccol("invD"),
            in1=s1, op0=ALU.mult, op1=ALU.subtract),
            reads=[st, self.sc], writes=[st])
        p.step("vector", lambda v: v.tensor_scalar(
            out=r, in0=var, scalar1=self.ccol("eps"), scalar2=None,
            op0=ALU.add), reads=[st, self.sc], writes=[st])
        self._rsqrt(r, s1, s2)
        rs = st[:, 8 * T + t:8 * T + t + 1]
        nb = st[:, 9 * T + t:9 * T + t + 1]
        if rms:
            rr = st[:, 7 * T + t:7 * T + t + 1]
            p.step("vector", lambda v: v.tensor_mul(out=rr, in0=var, in1=r),
                   reads=[st], writes=[st])
            p.step("vector", lambda v: v.tensor_mul(out=rr, in0=rr, in1=r),
                   reads=[st], writes=[st])
            p.step("vector", lambda v: v.tensor_scalar(
                out=rr, in0=rr, scalar1=self.ccol("rmseps"), scalar2=None,
                op0=ALU.add), reads=[st, self.sc], writes=[st])
            self._rsqrt(rr, s1, s2)
            p.step("vector", lambda v: v.tensor_mul(out=rs, in0=r, in1=rr),
                   reads=[st], writes=[st])
            p.step("vector", lambda v: v.scalar_tensor_tensor(
                out=st[:, 10 * T + t:10 * T + t + 1], in0=mu,
                scalar=self.ccol("negone"), in1=r,
                op0=ALU.mult, op1=ALU.mult), reads=[st, self.sc], writes=[st])
        else:
            p.step("vector", lambda v: v.tensor_copy(out=rs, in_=r),
                   reads=[st], writes=[st])
        p.step("vector", lambda v: v.scalar_tensor_tensor(
            out=nb, in0=mu, scalar=self.ccol("negone"), in1=rs,
            op0=ALU.mult, op1=ALU.mult), reads=[st, self.sc], writes=[st])

    def emit_stats_finish(self):
        cfg, p = self.cfg, self.p
        T = cfg.T
        st = self.stats
        mu = st[:, 2 * T:3 * T]
        r = st[:, 3 * T:4 * T]
        scr = st[:, 4 * T:5 * T]
        p.step("vector", lambda v: v.tensor_scalar(
            out=mu, in0=st[:, 0:T], scalar1=self.ccol("invD"), scalar2=None,
            op0=ALU.mult), reads=[st, self.sc], writes=[st])
        p.step("vector", lambda v: v.tensor_mul(out=scr, in0=mu, in1=mu),
               reads=[st], writes=[st])
        p.step("vector", lambda v: v.scalar_tensor_tensor(
            out=r, in0=st[:, T:2 * T], scalar=self.ccol("invD"), in1=scr,
            op0=ALU.mult, op1=ALU.subtract),
            reads=[st, self.sc], writes=[st])
        p.step("vector", lambda v: v.tensor_scalar(
            out=r, in0=r, scalar1=self.ccol("eps"), scalar2=None, op0=ALU.add),
            reads=[st, self.sc], writes=[st])
        self._rsqrt(r, st[:, 4 * T:5 * T], st[:, 5 * T:6 * T])

    def _rsqrt(self, veps, scr1, scr2):
        p = self.p
        st = self.stats
        p.step("scalar", lambda s: s.activation(out=scr1, in_=veps, func=AF.Sqrt),
               reads=[st], writes=[st])
        p.step("vector", lambda v: v.reciprocal(out=scr2, in_=scr1),
               reads=[st], writes=[st])
        p.step("vector", lambda v: v.tensor_mul(out=scr1, in0=scr2, in1=scr2),
               reads=[st], writes=[st])
        p.step("vector", lambda v: v.tensor_mul(out=scr1, in0=scr1, in1=veps),
               reads=[st], writes=[st])
        p.step("vector", lambda v: v.tensor_scalar(
            out=scr1, in0=scr1, scalar1=self.ccol("neghalf"),
            scalar2=self.ccol("half15"), op0=ALU.mult, op1=ALU.add),
            reads=[st, self.sc], writes=[st])
        p.step("vector", lambda v: v.tensor_mul(out=veps, in0=scr2, in1=scr1),
               reads=[st], writes=[st])

    def norm_slice(self, t, xs, lo, hi, dst_ap, dst_buf, eng="vector",
                   plain=False):
        cfg, p = self.cfg, self.p
        T = cfg.T
        st = self.stats
        sc_c = 3 * T + t if plain else 8 * T + t
        nb_c = 10 * T + t if plain else 9 * T + t
        if eng == "vector":
            p.step("vector", lambda v: v.tensor_scalar(
                out=dst_ap, in0=xs[:, lo:hi],
                scalar1=st[:, 2 * T + t:2 * T + t + 1],
                scalar2=st[:, sc_c:sc_c + 1],
                op0=ALU.subtract, op1=ALU.mult),
                reads=self.xr(t, lo, hi) + [st], writes=[dst_buf])
        else:
            p.step("scalar", lambda s: s.activation(
                out=dst_ap, in_=xs[:, lo:hi], func=AF.Identity,
                scale=st[:, sc_c:sc_c + 1],
                bias=st[:, nb_c:nb_c + 1]),
                reads=self.xr(t, lo, hi) + [st], writes=[dst_buf])

    def transpose_to(self, src, src_off, nchunks, dstbuf, dst_fn):
        p = self.p
        for cg in range(0, nchunks, 8):
            n = min(8, nchunks - cg)
            bank = 4 + 2 * ((cg // 8) % 2)

            def tf(te, cg=cg, n=n, bank=bank):
                last = None
                for k in range(n):
                    last = te.transpose(
                        out=self.pbb(bank)[:, k * 128:(k + 1) * 128],
                        in_=src[:, src_off + (cg + k) * 128:
                                src_off + (cg + k + 1) * 128],
                        identity=self.sid_b[:])
                return last
            p.step("tensor", tf, reads=[src, self.sid_b],
                   writes=[self.pb[bank]])
            p.step("vector", lambda v, cg=cg, n=n, bank=bank: v.tensor_copy(
                out=dst_fn(cg, n),
                in_=self.pbb(bank)[:, 0:n * 128].rearrange(
                    "p (k j) -> p k j", j=128)),
                reads=[self.pb[bank]], writes=[dstbuf])

    def emit_ln_to_actT(self, src=None):
        cfg = self.cfg
        NT = cfg.NT
        W = self.SLW
        nch = W // 128
        for t in range(cfg.T):
            xs = self.load_xtile(t, src)
            self.emit_tile_sums(t, xs)
            self.emit_tile_finish(t)
            for q in range(cfg.D // W):
                sg = self.stg[q % 4]
                self.norm_slice(t, xs, q * W, (q + 1) * W, sg[:, 0:W], sg,
                                eng=("vector", "scalar")[q % 2])
                self.transpose_to(
                    sg, 0, nch, self.actT,
                    lambda cg, n, t=t, q=q: self.actT[:].rearrange(
                        "p (c n) -> p c n", n=NT)[
                        :, nch * q + cg:nch * q + cg + n,
                        t * 128:(t + 1) * 128])

    def stream_w(self, wdram, s, fc, ow, wslot):
        p = self.p
        ns = min(8, fc)
        for i in range(ns):
            c0 = i * fc // ns
            c1 = (i + 1) * fc // ns
            if c0 >= c1:
                continue
            stgb = self.wstg[i % 4]
            p.dma_step("sync", None, stgb, stgb[:, 0:(c1 - c0) * ow],
                       wdram, wdram[s, :, c0 * ow:c1 * ow])
            p.step("vector", lambda v, stgb=stgb, c0=c0, c1=c1, wslot=wslot,
                   ow=ow: v.tensor_copy(out=self.wbf[wslot][:, c0 * ow:c1 * ow],
                                        in_=stgb[:, 0:(c1 - c0) * ow]),
                   reads=[stgb], writes=[self.wbf[wslot]])
        return self.wbf[wslot]

    def preload_w(self, wdram, nslices, ow=128, fc=None, s_base=0):
        FC = fc if fc is not None else self.cfg.FC
        for s in range(s_base, min(s_base + 2, s_base + nslices)):
            self.stream_w(wdram, s, FC, ow, s % 2)

    def emit_wproj(self, tag, wdram, nslices, evac, moving_ap_fn=None,
                   moving_bufs=None, ow=128, fc=None, extra_mm=None,
                   extra_reads=(), pre_slice=None, s_base=0, preloaded=False):
        cfg, p = self.cfg, self.p
        FC = fc if fc is not None else cfg.FC
        NT = cfg.NT
        if moving_ap_fn is None:
            def moving_ap_fn(c, p0, p1):
                return self.actT[:, c * NT + p0: c * NT + p1]
        if moving_bufs is None:
            moving_bufs = [self.actT]
        pend = None
        for s in range(s_base, s_base + nslices):
            if s == s_base:
                if not preloaded:
                    self.stream_w(wdram, s, FC, ow, s % 2)
                if pre_slice is not None:
                    pre_slice(s)
            if s + 1 < s_base + nslices:
                if not (preloaded and s + 1 == s_base + 1):
                    self.stream_w(wdram, s + 1, FC, ow, (s + 1) % 2)
                if pre_slice is not None:
                    pre_slice(s + 1)
            wb = self.wbf[s % 2]
            bA = self.pb[2 * (s % 2)]
            bB = self.pb[2 * (s % 2) + 1]

            def mmf(te, wb=wb, s=s, bA=bA, bB=bB):
                last = None
                for c in range(FC):
                    lhsT = wb[:, c * ow:(c + 1) * ow]
                    for pan in range(cfg.NPAN):
                        p0, p1 = pan * 512, min(NT, (pan + 1) * 512)
                        bank = (bA, bB)[pan]
                        last = te.matmul(out=bank[0:ow, 0:p1 - p0], lhsT=lhsT,
                                         rhs=moving_ap_fn(c, p0, p1),
                                         start=(c == 0),
                                         stop=(c == FC - 1 and extra_mm is None))
                if extra_mm is not None:
                    last = extra_mm(te, s, bA, bB)
                return last
            p.step("tensor", mmf,
                   reads=[wb] + list(moving_bufs) + list(extra_reads),
                   writes=[bA, bB])
            if pend is not None:
                evac(*pend)
            pend = (s, bA, bB)
        if pend is not None:
            evac(*pend)

    def emit_qkv(self, layer):
        cfg, p = self.cfg, self.p
        FC, NT, T = cfg.FC, cfg.NT, cfg.T
        if layer == 0:
            p.dma_step("sync", None, self.sbias, self.sbias[:, 0:3 * FC],
                       self.i["b_qkv0"], self.i["b_qkv0"].ap())

            def evac(s, bA, bB):
                kind = s // FC
                h = s % FC
                stg = self.stg[0] if kind < 2 else self.stg[1]
                for pan in range(cfg.NPAN):
                    p0, p1 = pan * 512, min(NT, (pan + 1) * 512)
                    bank = (bA, bB)[pan]
                    p.step("scalar", lambda sc, bank=bank, stg=stg, p0=p0, p1=p1,
                           s=s: sc.activation(
                               out=stg[:, p0:p1], in_=bank[:, 0:p1 - p0],
                               func=AF.Identity, bias=self.sbias[:, s:s + 1]),
                           reads=[bank, self.sbias], writes=[stg])
                if kind == 0:
                    dst = self.d["q0"]
                    p.dma_step("scalar", None, dst, dst[h * 128:(h + 1) * 128, :],
                               stg, stg[:, 0:NT])
                elif kind == 1:
                    dst = self.d["k0own"]
                    p.dma_step("scalar", f"kst{h % 2}", self.d["k0ownH"][h],
                               dst[h * 128:(h + 1) * 128, :],
                               stg, stg[:, 0:NT])
                    self.emit_gather_head("k0", h, 128, 256)
                else:
                    self.v_transpose_store(stg, h, self.d["v0own"],
                                           out_buf=self.d["v0ownH"][h],
                                           ch=f"vst{h % 2}")
                    self.emit_gather_head("v0", h, NT, 2 * NT)
            self.emit_wproj("qkv0a", self.i["w_qkv0"], 2 * FC, evac,
                            preloaded=True)
            self.emit_wproj("qkv0b", self.i["w_qkv0"], FC, evac, s_base=2 * FC)
        else:
            self.emit_wproj("q1", self.i["w_q1"], FC,
                            lambda s, bA, bB: self.rope_evac(s, bA, bB,
                                                             self.d["q1"]))
            def evack(s, bA, bB):
                self.rope_evac(s, bA, bB, self.d["k1own"],
                               out_buf=self.d["k1ownH"][s], ch=f"kst{s % 2}")
                self.emit_gather_head("k1", s, 128, 256)
            self.emit_wproj("k1", self.i["w_k1"], cfg.HKV1, evack)

            def evacv(s, bA, bB):
                stg = self.stg[1]
                for pan in range(cfg.NPAN):
                    p0, p1 = pan * 512, min(NT, (pan + 1) * 512)
                    bank = (bA, bB)[pan]
                    p.step("scalar", lambda sc, bank=bank, p0=p0, p1=p1:
                           sc.copy(out=stg[:, p0:p1], in_=bank[:, 0:p1 - p0]),
                           reads=[bank], writes=[stg])
                self.v_transpose_store(stg, s, self.d["v1own"],
                                       out_buf=self.d["v1ownH"][s],
                                       ch=f"vst{s % 2}")
                self.emit_gather_head("v1", s, NT, 2 * NT)
            self.emit_wproj("v1", self.i["w_v1"], cfg.HKV1, evacv)

    def rope_evac(self, h, bA, bB, dst, out_buf=None, ch=None):
        cfg, p = self.cfg, self.p
        NT = cfg.NT
        stgP, stgA, stgC = self.stg[0], self.stg[1], self.stg[2]
        rotA, rotB = self.pb[4], self.pb[5]
        for pan in range(cfg.NPAN):
            p0, p1 = pan * 512, min(NT, (pan + 1) * 512)
            bank = (bA, bB)[pan]
            p.step("scalar", lambda sc, bank=bank, p0=p0, p1=p1:
                   sc.copy(out=stgP[:, p0:p1], in_=bank[:, 0:p1 - p0]),
                   reads=[bank], writes=[stgP])
        def rmm(te):
            last = None
            for pan in range(cfg.NPAN):
                p0, p1 = pan * 512, min(NT, (pan + 1) * 512)
                bank = (rotA, rotB)[pan]
                last = te.matmul(out=bank[:, 0:p1 - p0], lhsT=self.sprot[:],
                                 rhs=stgP[:, p0:p1], start=True, stop=True)
            return last
        p.step("tensor", rmm, reads=[stgP, self.sprot], writes=[rotA, rotB])
        p.step("gpsimd", lambda g: g.tensor_mul(
            out=stgC[:, 0:NT], in0=stgP[:, 0:NT], in1=self.scos[:, 0:NT]),
            reads=[stgP, self.scos], writes=[stgC])
        for pan in range(cfg.NPAN):
            p0, p1 = pan * 512, min(NT, (pan + 1) * 512)
            bank = (rotA, rotB)[pan]
            p.step("vector", lambda v, bank=bank, p0=p0, p1=p1: v.tensor_mul(
                out=stgA[:, p0:p1], in0=bank[:, 0:p1 - p0],
                in1=self.ssin[:, p0:p1]),
                reads=[bank, self.ssin], writes=[stgA])
        p.step("vector", lambda v: v.tensor_add(
            out=stgC[:, 0:NT], in0=stgC[:, 0:NT], in1=stgA[:, 0:NT]),
            reads=[stgC, stgA], writes=[stgC])
        p.dma_step("scalar", ch, out_buf if out_buf is not None else dst,
                   dst[h * 128:(h + 1) * 128, :],
                   stgC, stgC[:, 0:NT])

    def v_transpose_store(self, stg, h, dst, out_buf=None, ch=None):
        cfg, p = self.cfg, self.p
        T, NT = cfg.T, cfg.NT
        stgT = self.stg[3]
        self.transpose_to(stg, 0, T, stgT,
                          lambda k0, n: stgT[:, k0 * 128:(k0 + n) * 128].rearrange(
                              "p (k j) -> p k j", j=128))
        p.dma_step("scalar", ch, out_buf if out_buf is not None else dst,
                   dst[h * NT:(h + 1) * NT, :].rearrange(
                       "(t p) j -> p t j", p=128),
                   stgT, stgT[:, 0:NT].rearrange("p (t j) -> p t j", j=128))

    def emit_gather_head(self, nm, h, ro, rf):
        p = self.p
        rg = [[c, c + 1] for c in range(0, self.cfg.n_cores, 2)]
        srcb = self.d[nm + "ownH"][h]
        dstb = self.d[nm + "fullH"][h]
        src_ap = self.d[nm + "own"].h[h * ro:(h + 1) * ro, :]
        dst_ap = self.d[nm + "full"].h[h * rf:(h + 1) * rf, :]
        p.step("gpsimd", lambda g, src_ap=src_ap, dst_ap=dst_ap:
               g.collective_compute(
                   "AllGather", ALU.bypass, replica_groups=rg,
                   ins=[src_ap], outs=[dst_ap]),
               reads=[srcb], writes=[dstb])

    def emit_gather_one(self, a, b):
        p = self.p
        rg = [[c, c + 1] for c in range(0, self.cfg.n_cores, 2)]
        srcb, dstb = self.d[a], self.d[b]
        p.step("gpsimd", lambda g, srcb=srcb, dstb=dstb: g.collective_compute(
            "AllGather", ALU.bypass, replica_groups=rg,
            ins=[srcb.ap()], outs=[dstb.ap()]),
            reads=[srcb], writes=[dstb])

    def emit_attention(self, layer):
        cfg, p = self.cfg, self.p
        T, NT, FC = cfg.T, cfg.NT, cfg.FC
        causal = (layer == 1)
        nkv = cfg.HKV1 if layer else cfg.H0
        gq = (cfg.H1 // cfg.HKV1) if layer else 1
        nheads = nkv * gq
        kfull = self.d["k1full" if layer else "k0full"]
        vfull = self.d["v1full" if layer else "v0full"]
        kfH = self.d[("k1" if layer else "k0") + "fullH"]
        vfH = self.d[("v1" if layer else "v0") + "fullH"]
        qd = self.d["q1" if layer else "q0"]
        DKV = cfg.DKV1 if layer else cfg.D
        negC = self.ccol("negC1" if layer else "negC0")
        masks = {0: self.smask0, 1: self.smask1}

        for kv in range(nkv):
            kb = self.aK[kv % 2]
            vb = self.aV[kv % 2]
            for r in range(2):
                p.dma_step("sync", None, kb, kb[:, r * NT:(r + 1) * NT],
                           kfH[kv], kfull[kv * 256 + r * 128:
                                          kv * 256 + (r + 1) * 128, :])
                p.dma_step("sync", None, vb,
                           vb[:, r * NT:(r + 1) * NT].rearrange(
                               "p (t j) -> p t j", j=128),
                           vfH[kv], vfull[(kv * 2 + r) * NT:
                                          (kv * 2 + r + 1) * NT, :].rearrange(
                               "(t p) j -> p t j", p=128))
            for qi in range(gq):
                h = kv * gq + qi
                qb = self.aQ[h % 2]
                p.dma_step("sync", None, qb, qb.ap(),
                           qd, qd[h * 128:(h + 1) * 128, :])
                ctxA, ctxB = self.pb[4], self.pb[5]
                lA, lB = self.pb[6], self.pb[7]
                slots = []
                for m in range(T):
                    for r in range(2):
                        qoff = m * 128 if causal else 0
                        if qoff < NT:
                            slots.append((r, m, qoff))
                pan_slots = {pan: [] for pan in range(cfg.NPAN)}
                for si, (r, m, qoff) in enumerate(slots):
                    for pan in range(cfg.NPAN):
                        p0 = max(pan * 512, qoff)
                        p1 = min(NT, (pan + 1) * 512)
                        if p0 < p1:
                            pan_slots[pan].append(si)

                def emit_smm(si):
                    r, m, qoff = slots[si]
                    sbA = self.pb[2 * (si % 2)]
                    sbB = self.pb[2 * (si % 2) + 1]
                    sl = r * T + m

                    def smm(te, sl=sl, sbA=sbA, sbB=sbB, qoff=qoff,
                            kb=kb, qb=qb):
                        last = None
                        for pan in range(cfg.NPAN):
                            p0 = max(pan * 512, qoff)
                            p1 = min(NT, (pan + 1) * 512)
                            if p0 >= p1:
                                continue
                            sb = (sbA, sbB)[pan]
                            last = te.matmul(
                                out=sb[:, p0 - pan * 512: p1 - pan * 512],
                                lhsT=kb[:, sl * 128:(sl + 1) * 128],
                                rhs=qb[:, p0:p1], start=True, stop=True)
                        return last
                    p.step("tensor", smm, reads=[kb, qb], writes=[sbA, sbB])

                def emit_tail(si):
                    r, m, qoff = slots[si]
                    spq = self.pq[si % 2]
                    eb = self.aE[si % 2]
                    sl = r * T + m
                    p.step("scalar", lambda sc, spq=spq, eb=eb, qoff=qoff:
                           sc.activation(
                               out=eb[:, qoff:NT], in_=spq.h[:, qoff:NT],
                               func=AF.Exp, bias=negC,
                               scale=self.ccol("scale0")),
                           reads=[spq, self.sc], writes=[eb])
                    if causal:
                        mk = masks[r]
                        p.step("vector", lambda v, eb=eb, m=m, mk=mk: v.tensor_mul(
                            out=eb[:, m * 128:(m + 1) * 128],
                            in0=eb[:, m * 128:(m + 1) * 128], in1=mk[:]),
                            reads=[eb, mk], writes=[eb])

                    def pvmm(te, si=si, sl=sl, eb=eb, qoff=qoff, vb=vb):
                        last = None
                        for lmm in range(2):
                            for pan in range(cfg.NPAN):
                                p0 = max(pan * 512, qoff)
                                p1 = min(NT, (pan + 1) * 512)
                                if p0 >= p1:
                                    continue
                                fs = (si == pan_slots[pan][0])
                                st_ = (si == pan_slots[pan][-1])
                                if lmm == 0:
                                    ctx = (ctxA, ctxB)[pan]
                                    last = te.matmul(
                                        out=ctx[:, p0 - pan * 512:
                                                p1 - pan * 512],
                                        lhsT=vb[:, sl * 128:(sl + 1) * 128],
                                        rhs=eb[:, p0:p1], start=fs, stop=st_)
                                else:
                                    lb_ = (lA, lB)[pan]
                                    last = te.matmul(
                                        out=lb_[0:1, p0 - pan * 512:
                                                p1 - pan * 512],
                                        lhsT=self.sones[:], rhs=eb[:, p0:p1],
                                        start=fs, stop=st_)
                        return last
                    p.step("tensor", pvmm, reads=[eb, vb, self.sones],
                           writes=[ctxA, ctxB, lA, lB])

                emit_smm(0)
                for si in range(len(slots)):
                    if si + 1 < len(slots):
                        emit_smm(si + 1)
                    emit_tail(si)
                p.step("vector", lambda v, h=h: v.tensor_copy(
                    out=self.actT[:, h * NT: h * NT + NT],
                    in_=self.pq[2].h[:, 0:NT]),
                    reads=[ctxA, ctxB], writes=[self.actT])
                p.step("vector", lambda v: v.tensor_copy(
                    out=self.lrows[0:1, 0:NT], in_=self.pq[3].h[0:1, 0:NT]),
                    reads=[lA, lB], writes=[self.lrows])
                p.dma_step("scalar", None, self.d["lpk"],
                           self.d["lpk"][0:1, h * NT:(h + 1) * NT],
                           self.lrows, self.lrows[0:1, 0:NT])
        p.dma_step("scalar", None, self.lrows, self.lrows[0:nheads, 0:NT],
                   self.d["lpk"], self.d["lpk"][0:1, 0:nheads * NT].rearrange(
                       "p (h n) -> (p h) n", n=NT))
        p.step("vector", lambda v: v.reciprocal(
            out=self.lrows[0:nheads, 0:NT], in_=self.lrows[0:nheads, 0:NT]),
            reads=[self.lrows], writes=[self.lrows])
        p.dma_step("scalar", None, self.d["lpk2"],
                   self.d["lpk2"][0:1, 0:nheads * NT].rearrange(
                       "p (h n) -> (p h) n", n=NT),
                   self.lrows, self.lrows[0:nheads, 0:NT])
        for h in range(nheads):
            p.dma_step("scalar", None, self.lrows, self.lrows[0:1, 0:NT],
                       self.d["lpk2"],
                       self.d["lpk2"][0:1, h * NT:(h + 1) * NT])
            bkA = self.pb[2 * (h % 2)]
            bkB = self.pb[2 * (h % 2) + 1]

            def bc(te, bkA=bkA, bkB=bkB):
                last = None
                for pan in range(cfg.NPAN):
                    p0, p1 = pan * 512, min(NT, (pan + 1) * 512)
                    bk = (bkA, bkB)[pan]
                    last = te.matmul(out=bk[:, 0:p1 - p0], lhsT=self.sonesr[:],
                                     rhs=self.lrows[0:1, p0:p1],
                                     start=True, stop=True)
                return last
            p.step("tensor", bc, reads=[self.lrows, self.sonesr],
                   writes=[bkA, bkB])
            p.step("vector", lambda v, h=h: v.tensor_mul(
                out=self.actT[:, h * NT: h * NT + NT],
                in0=self.actT[:, h * NT: h * NT + NT],
                in1=self.pq[h % 2].h[:, 0:NT]),
                reads=[self.actT, bkA, bkB], writes=[self.actT])

    def emit_outproj(self, tag, wdram, bias_dram, extra_mm=None,
                     extra_reads=(), pre_slice=None, dst="X"):
        cfg, p = self.cfg, self.p
        T, NT, FC = cfg.T, cfg.NT, cfg.FC
        if bias_dram is not None:
            p.dma_step("sync", None, self.sbias, self.sbias[:, 0:FC],
                       bias_dram, bias_dram.ap())

        def evac(s, bA, bB):
            stgE = self.stg[2]
            stgf = (self.stgF, self.lrows)[s % 2]
            for pan in range(cfg.NPAN):
                p0, p1 = pan * 512, min(NT, (pan + 1) * 512)
                bank = (bA, bB)[pan]
                if bias_dram is not None:
                    p.step("scalar", lambda sc, bank=bank, p0=p0, p1=p1, s=s:
                           sc.activation(out=stgE[:, p0:p1],
                                         in_=bank[:, 0:p1 - p0], func=AF.Identity,
                                         bias=self.sbias[:, s:s + 1]),
                           reads=[bank, self.sbias], writes=[stgE])
                else:
                    p.step("scalar", lambda sc, bank=bank, p0=p0, p1=p1:
                           sc.copy(out=stgE[:, p0:p1], in_=bank[:, 0:p1 - p0]),
                           reads=[bank], writes=[stgE])
            for tg in range(0, T, 8):
                n = min(8, T - tg)
                bank = 4 + 2 * ((tg // 8) % 2)

                def tf(te, tg=tg, n=n, bank=bank):
                    last = None
                    for k in range(n):
                        last = te.transpose(
                            out=self.pbb(bank)[:, k * 128:(k + 1) * 128],
                            in_=stgE[:, (tg + k) * 128:(tg + k + 1) * 128],
                            identity=self.sid_b[:])
                    return last
                p.step("tensor", tf, reads=[stgE, self.sid_b],
                       writes=[self.pb[bank]])
                p.step("vector", lambda v, bank=bank, n=n, stgf=stgf:
                       v.tensor_copy(
                           out=stgf[:, 0:n * 128],
                           in_=self.pbb(bank)[:, 0:n * 128]),
                       reads=[self.pb[bank]], writes=[stgf])
                chunk = (s * 128) // self.SLW
                if dst == "X":
                    ob = [self.d["Xc"][tg + k][chunk] for k in range(n)]
                    oap, cname = self.Xap, f"xacc{chunk}"
                else:
                    ob = [self.i["Yt"][tg + k] for k in range(n)]
                    oap, cname = self.Yap, f"yacc{chunk}"
                tok = p.dma_step(
                    "gpsimd", cname, ob,
                    oap[:].rearrange("(t p) d -> p t d", p=128)[
                        :, tg:tg + n, s * 128:(s + 1) * 128],
                    stgf,
                    stgf[:, 0:n * 128].rearrange(
                        "p (k j) -> p k j", j=128),
                    accum_op=ALU.add)
                if dst != "X":
                    self.final_deps.append(tok)
        self.emit_wproj(tag, wdram, FC, evac, extra_mm=extra_mm,
                        extra_reads=extra_reads, pre_slice=pre_slice)

    def emit_l0ffn(self):
        cfg, p = self.cfg, self.p
        T, NT, FC, C, FCC = cfg.T, cfg.NT, cfg.FC, cfg.C, cfg.FCC
        D = cfg.D
        st = self.stats
        p.step("vector", lambda v: v.tensor_mul(
            out=self.sbases[:, 1:2], in0=self.sbases[:, 0:1],
            in1=self.sc[0:24, CN["tau"]:CN["tau"] + 1]),
            reads=[self.sbases, self.sc], writes=[self.sbases])
        p.step("scalar", lambda s: s.activation(
            out=self.sgate[:, 1:2], in_=self.sgate[:, 0:1], func=AF.Sigmoid),
            reads=[self.sgate], writes=[self.sgate])
        WS = 512
        for j in range((FC * 24 + WS - 1) // WS):
            c0, c1 = j * WS, min(FC * 24, (j + 1) * WS)
            stgb = self.wstg[j % 4]
            p.dma_step("sync", None, stgb, stgb[:, 0:c1 - c0],
                       self.i["w_res0"], self.i["w_res0"][0, :, c0:c1])
            p.step("scalar", lambda sc, stgb=stgb, c0=c0, c1=c1: sc.copy(
                out=self.wbf[0][:, c0:c1], in_=stgb[:, 0:c1 - c0]),
                reads=[stgb], writes=[self.wbf[0]])
        W = self.SLW
        nsl = D // W
        nch = W // 128
        for t in range(T):
            xs = self.load_xtile(t)
            self.emit_tile_sums(t, xs)
            self.emit_tile_finish(t, rms=True)
            for q in range(nsl):
                sg = self.stg[q % 4]
                self.norm_slice(t, xs, q * W, (q + 1) * W, sg[:, 0:W],
                                sg, eng=("vector", "scalar")[q % 2])
            for q in range(nsl):
                sg = self.stg[q % 4]
                self.transpose_to(
                    sg, 0, nch, self.wbf[1],
                    lambda cg, n, q=q: self.wbf[1][
                        :, (nch * q + cg) * 128:
                        (nch * q + cg + n) * 128].rearrange(
                        "p (k j) -> p k j", j=128))
            lgbank = self.pb[2]

            def lgmm(te, t=t, lgbank=lgbank):
                last = None
                for c in range(FC):
                    last = te.matmul(
                        out=lgbank[0:24, 0:128],
                        lhsT=self.wbf[0][:, c * 24:(c + 1) * 24],
                        rhs=self.wbf[1][:, c * 128:(c + 1) * 128],
                        start=(c == 0), stop=(c == FC - 1))
                return last
            p.step("tensor", lgmm, reads=[self.wbf[1], self.wbf[0]],
                   writes=[lgbank])
            p.step("scalar", lambda s, lgbank=lgbank: s.activation(
                out=self.lgt[0:24, 0:128],
                in_=lgbank[0:24, 0:128], func=AF.Identity,
                bias=self.sbases[:, 1:2],
                scale=self.sc[0:24, CN["tau"]:CN["tau"] + 1]),
                reads=[lgbank, self.sbases, self.sc], writes=[self.lgt])
            p.step("tensor", lambda te: te.transpose(
                out=self.pb[6][:, 0:24],
                in_=self.lgt[0:24, 0:128],
                identity=self.sid_f[0:24, 0:24]),
                reads=[self.lgt, self.sid_f], writes=[self.pb[6]])
            p.step("vector", lambda v, t=t: v.tensor_copy(
                out=self.Ltm[:, t * 24:(t + 1) * 24],
                in_=self.pb[6][:, 0:24]),
                reads=[self.pb[6]], writes=[self.Ltm])
        Lv = self.Ltm[:].rearrange("p (t k) -> p t k", k=24)
        for off in (16, 20):
            sl = Lv[:, :, off:off + 4]
            m1 = self.sinkM[:, 0:T].rearrange("p (t o) -> p t o", o=1)
            p.step("vector", lambda v, sl=sl, m1=m1: v.reduce_max(
                out=m1, in_=sl, axis=AX.X),
                reads=[self.Ltm], writes=[self.sinkM])
            mb = m1.broadcast_to([128, T, 4])
            p.step("vector", lambda v, sl=sl, mb=mb: v.tensor_sub(
                out=sl, in0=sl, in1=mb),
                reads=[self.Ltm, self.sinkM], writes=[self.Ltm])
            p.step("scalar", lambda s, sl=sl: s.activation(
                out=sl, in_=sl, func=AF.Exp),
                reads=[self.Ltm], writes=[self.Ltm])
            p.step("vector", lambda v, sl=sl, m1=m1: v.reduce_sum(
                out=m1, in_=sl, axis=AX.X),
                reads=[self.Ltm], writes=[self.sinkM])
            p.step("vector", lambda v: v.reciprocal(
                out=self.sinkM[:, T:2 * T], in_=self.sinkM[:, 0:T]),
                reads=[self.sinkM], writes=[self.sinkM])
            rb = self.sinkM[:, T:2 * T].rearrange(
                "p (t o) -> p t o", o=1).broadcast_to([128, T, 4])
            p.step("vector", lambda v, sl=sl, rb=rb: v.tensor_mul(
                out=sl, in0=sl, in1=rb),
                reads=[self.Ltm, self.sinkM], writes=[self.Ltm])
        halves = []
        T2 = (T + 1) // 2
        for hx, (t0, t1) in enumerate([(0, T2), (T2, T)]):
            if t0 >= t1:
                continue
            nt = t1 - t0
            bE, bP = self.sinkEh[hx], self.sinkPh[hx]
            bU, bV, bM = self.sinkUh[hx], self.sinkVh[hx], self.sinkMh[hx]
            Zh = Lv[:, t0:t1, 0:16].rearrange("p t (i j) -> p t i j", j=4)
            E4 = bE[:, 0:nt * 16].rearrange("p (t i j) -> p t i j", i=4, j=4)
            P4 = bP[:, 0:nt * 16].rearrange("p (t i j) -> p t i j", i=4, j=4)
            P4T = bP[:, 0:nt * 16].rearrange("p (t i j) -> p t j i", i=4, j=4)
            u2 = bU[:, 0:nt * 4].rearrange("p (t i) -> p t i", i=4)
            v2 = bV[:, 0:nt * 4].rearrange("p (t j) -> p t j", j=4)
            sr = bM[:, 0:nt * 4].rearrange("p (t i) -> p t i", i=4)
            sc2 = bM[:, 4 * T2:4 * T2 + nt * 4].rearrange(
                "p (t j) -> p t j", j=4)
            cb = v2.unsqueeze(2).broadcast_to([128, nt, 4, 4])
            rb = u2.unsqueeze(3).broadcast_to([128, nt, 4, 4])
            sr0 = sr[:, :, 0:1].broadcast_to([128, nt, 4])
            sc0 = sc2[:, :, 0:1].broadcast_to([128, nt, 4])
            mb0 = sr.unsqueeze(3).broadcast_to([128, nt, 4, 4])
            ops = []

            def add(eng, fn, reads, writes, ops=ops):
                ops.append((eng, fn, reads, writes))

            add("vector", lambda v, Zh=Zh, sr=sr: v.reduce_max(
                out=sr, in_=Zh, axis=AX.X), [self.Ltm], [bM])
            add("vector", lambda v, Zh=Zh, E4=E4, mb0=mb0: v.tensor_sub(
                out=E4, in0=Zh, in1=mb0), [self.Ltm, bM], [bE])
            add("scalar", lambda s, bE=bE, nt=nt: s.activation(
                out=bE[:, 0:nt * 16], in_=bE[:, 0:nt * 16], func=AF.Exp),
                [bE], [bE])
            add("vector", lambda v, bV=bV, nt=nt: v.memset(
                bV[:, 0:nt * 4], 1.0), [], [bV])
            for it in range(cfg.sink_iters):
                last = it == cfg.sink_iters - 1
                resc = it < 4 or it % 4 == 0
                add("vector", lambda v, P4=P4, E4=E4, cb=cb: v.tensor_mul(
                    out=P4, in0=E4, in1=cb), [bE, bV], [bP])
                add("vector", lambda v, sr=sr, P4=P4: v.reduce_sum(
                    out=sr, in_=P4, axis=AX.X), [bP], [bM])
                eng_r = "vector"
                add(eng_r, lambda v, bU=bU, bM=bM, nt=nt: v.reciprocal(
                    out=bU[:, 0:nt * 4], in_=bM[:, 0:nt * 4]), [bM], [bU])
                if resc:
                    add("vector", lambda v, u2=u2, sr0=sr0: v.tensor_mul(
                        out=u2, in0=u2, in1=sr0), [bU, bM], [bU])
                add("vector", lambda v, P4=P4, E4=E4, rb=rb: v.tensor_mul(
                    out=P4, in0=E4, in1=rb), [bE, bU], [bP])
                add("vector", lambda v, sc2=sc2, P4T=P4T: v.reduce_sum(
                    out=sc2, in_=P4T, axis=AX.X), [bP], [bM])
                add(eng_r, lambda v, bV=bV, bM=bM, nt=nt: v.reciprocal(
                    out=bV[:, 0:nt * 4],
                    in_=bM[:, 4 * T2:4 * T2 + nt * 4]), [bM], [bV])
                if not last and resc:
                    add("vector", lambda v, v2=v2, sc0=sc0: v.tensor_mul(
                        out=v2, in0=v2, in1=sc0), [bV, bM], [bV])
            add("vector", lambda v, P4=P4, E4=E4, rb=rb: v.tensor_mul(
                out=P4, in0=E4, in1=rb), [bE, bU], [bP])
            add("vector", lambda v, E4=E4, P4=P4, cb=cb: v.tensor_mul(
                out=E4, in0=P4, in1=cb), [bP, bV], [bE])
            halves.append(ops)
        for k in range(max(len(o) for o in halves)):
            for ops in halves:
                if k < len(ops):
                    eng, fn, rd, wr = ops[k]
                    p.step(eng, fn, reads=rd, writes=wr)
        p.step("tensor", lambda te: te.matmul(
            out=self.pb[6][:, 0:1], lhsT=self.sonesr[:],
            rhs=self.sgate[0:1, 1:2], start=True, stop=True),
            reads=[self.sgate, self.sonesr], writes=[self.pb[6]])
        p.step("vector", lambda v: v.tensor_copy(out=st[:, 4 * T:4 * T + 1],
                                                 in_=self.pb[6][:, 0:1]),
               reads=[self.pb[6]], writes=[st])
        p.step("vector", lambda v: v.tensor_scalar(
            out=st[:, 4 * T + 1:4 * T + 2], in0=st[:, 4 * T:4 * T + 1],
            scalar1=self.ccol("negone"),
            scalar2=self.ccol("one"), op0=ALU.mult, op1=ALU.add),
            reads=[st, self.sc], writes=[st])
        T2 = (T + 1) // 2
        for hx, (t0, t1) in enumerate([(0, T2), (T2, T)]):
            if t0 >= t1:
                continue
            nt = t1 - t0
            bE = self.sinkEh[hx]
            p.step("vector", lambda v, t0=t0, t1=t1, bE=bE, nt=nt:
                   v.tensor_scalar(
                       out=Lv[:, t0:t1, 0:16],
                       in0=bE[:, 0:nt * 16].rearrange("p (t k) -> p t k", k=16),
                       scalar1=st[:, 4 * T:4 * T + 1], scalar2=None, op0=ALU.mult),
                   reads=[bE, st], writes=[self.Ltm])
        diag = Lv[:, :, 0:16:5]
        p.step("vector", lambda v, diag=diag: v.tensor_scalar(
            out=diag, in0=diag, scalar1=st[:, 4 * T + 1:4 * T + 2],
            scalar2=None,
            op0=ALU.add),
            reads=[self.Ltm, st], writes=[self.Ltm])
        for t in range(T):
            xs = self.load_xtile(t)
            for n in range(cfg.NS):
                sg = self.stg[n % 2]
                self.norm_slice(t, xs, n * C, (n + 1) * C, sg[:, 0:C], sg,
                                eng=("vector", "scalar")[n % 2], plain=True)
                hc = self.Ltm[:, t * 24 + 16 + n: t * 24 + 17 + n]
                if n == 0:
                    p.step("vector", lambda v, sg=sg, hc=hc: v.tensor_scalar(
                        out=self.xpb[:, 0:C], in0=sg[:, 0:C],
                        scalar1=hc, scalar2=None, op0=ALU.mult),
                        reads=[sg, self.Ltm], writes=[self.xpb])
                else:
                    p.step("vector", lambda v, sg=sg, hc=hc:
                           v.scalar_tensor_tensor(
                               out=self.xpb[:, 0:C], in0=sg[:, 0:C],
                               scalar=hc, in1=self.xpb[:, 0:C],
                               op0=ALU.mult, op1=ALU.add),
                           reads=[sg, self.Ltm, self.xpb], writes=[self.xpb])
            self.transpose_to(
                self.xpb, 0, FCC, self.xpT,
                lambda cg, n8, t=t: self.xpT[:].rearrange(
                    "p (c n) -> p c n", n=NT)[:, cg:cg + n8,
                                              t * 128:(t + 1) * 128])
        def evy(s, bA, bB):
            stgy = self.stg[2]
            for pan in range(cfg.NPAN):
                p0, p1 = pan * 512, min(NT, (pan + 1) * 512)
                bank = (bA, bB)[pan]
                p.step("scalar", lambda sc, bank=bank, p0=p0, p1=p1:
                       sc.copy(out=stgy[:, p0:p1], in_=bank[:, 0:p1 - p0]),
                       reads=[bank], writes=[stgy])
            self.v_transpose_store(stgy, s, self.d["y0"])
        self.emit_wproj("ybr", self.i["w_br0"], FCC, evy,
                        moving_ap_fn=lambda c, p0, p1: self.xpT[
                            :, c * NT + p0: c * NT + p1],
                        moving_bufs=[self.xpT], fc=FCC)
        for t in range(T):
            xs = self.load_xtile(t)
            for jS in range(cfg.NS):
                self.norm_slice(t, xs, jS * C, (jS + 1) * C,
                                self.stg[jS][:, 0:C], self.stg[jS],
                                eng=("vector", "scalar")[jS % 2], plain=True)
            p.dma_step("scalar", None, self.ytile,
                       self.ytile[:, 0:C].rearrange("p (s j) -> p s j", j=128),
                       self.d["y0"],
                       self.d["y0"][:].rearrange(
                           "(s t p) j -> p s t j", p=128, t=T)[:, :, t, :])
            accs = [self.accb[0], self.accb[1]]
            engs = ["vector", "vector"]
            for iS0 in range(0, cfg.NS, 2):
                pair = [iS0, iS0 + 1]
                for jS in range(cfg.NS):
                    for k2, iS in enumerate(pair):
                        acc = accs[k2]
                        eng = engs[k2]
                        cc = self.Ltm[:, t * 24 + 4 * iS + jS:
                                      t * 24 + 4 * iS + jS + 1]
                        sg = self.stg[jS]
                        if jS == 0:
                            p.step(eng, lambda v, sg=sg, cc=cc, acc=acc:
                                   v.tensor_scalar(
                                       out=acc[:, 0:C], in0=sg[:, 0:C],
                                       scalar1=cc, scalar2=None, op0=ALU.mult),
                                   reads=[sg, self.Ltm], writes=[acc])
                        else:
                            p.step(eng, lambda v, sg=sg, cc=cc, acc=acc:
                                   v.scalar_tensor_tensor(
                                       out=acc[:, 0:C], in0=sg[:, 0:C],
                                       scalar=cc, in1=acc[:, 0:C],
                                       op0=ALU.mult, op1=ALU.add),
                                   reads=[sg, self.Ltm, acc], writes=[acc])
                for k2, iS in enumerate(pair):
                    acc = accs[k2]
                    eng = engs[k2]
                    hp = self.Ltm[:, t * 24 + 20 + iS: t * 24 + 21 + iS]
                    p.step(eng, lambda v, hp=hp, acc=acc:
                           v.scalar_tensor_tensor(
                               out=acc[:, 0:C], in0=self.ytile[:, 0:C],
                               scalar=hp, in1=acc[:, 0:C],
                               op0=ALU.mult, op1=ALU.add),
                           reads=[self.ytile, self.Ltm, acc], writes=[acc])
                for k2, iS in enumerate(pair):
                    acc = accs[k2]
                    p.step("vector", lambda v, iS=iS, acc=acc, xs=xs:
                           v.tensor_add(
                               out=xs[:, iS * C:(iS + 1) * C],
                               in0=xs[:, iS * C:(iS + 1) * C],
                               in1=acc[:, 0:C]),
                           reads=[acc] + self.xr(t, iS * C, (iS + 1) * C),
                           writes=self.xr(t, iS * C, (iS + 1) * C))
            W2 = self.SLW
            for q in range(self.NXCH):
                p.dma_step("scalar", f"xwb{q}", self.d["Xc"][t][q],
                           self.Xap[t * 128:(t + 1) * 128,
                                    q * W2:(q + 1) * W2],
                           self.xsc[t % 2][q], xs[:, q * W2:(q + 1) * W2])

    def emit_l1ffn(self):
        cfg, p = self.cfg, self.p
        T, NT, FC = cfg.T, cfg.NT, cfg.FC
        st = self.stats
        self.emit_ln_to_actT()
        for t in range(T):
            tok = p.dma_step(
                "sync", f"yt{t % 4}", self.i["Yt"][t],
                self.Yap[t * 128:(t + 1) * 128, :],
                self.d["Xc"][t], self.Xap[t * 128:(t + 1) * 128, :])
            self.final_deps.append(tok)
        for c in range(FC):
            stgb = self.wstg[c % 4]
            p.dma_step("sync", None, stgb, stgb[:, 0:512],
                       self.i["w_qk1"], self.i["w_qk1"][0, :, c * 512:(c + 1) * 512])
            wqb = self.wbf[c % 2]
            p.step("scalar", lambda sc, stgb=stgb, wqb=wqb: sc.copy(
                out=wqb[:, 0:512], in_=stgb[:, 0:512]),
                reads=[stgb], writes=[wqb])

            def qkmm(te, c=c, wqb=wqb):
                last = None
                for t in range(T):
                    last = te.matmul(
                        out=self.pb[t][:, 0:512],
                        lhsT=self.actT[:, c * NT + t * 128: c * NT + (t + 1) * 128],
                        rhs=wqb[:, 0:512], start=(c == 0), stop=(c == FC - 1))
                return last
            p.step("tensor", qkmm, reads=[wqb, self.actT],
                   writes=list(self.pb[0:T]))
        for t in range(T):
            p.step("scalar", lambda s, t=t: s.copy(
                out=self.qk_tm[:, (t % 2) * 512:(t % 2) * 512 + 512],
                in_=self.pb[t][:, 0:512]),
                reads=[self.pb[t]], writes=[self.qk_tm])
            q = self.qk_tm[:, (t % 2) * 512: (t % 2) * 512 + 256]
            k = self.qk_tm[:, (t % 2) * 512 + 256: (t % 2) * 512 + 512]
            p.step("vector", lambda v, q=q, k=k: v.tensor_mul(
                out=self.lrows[:, 0:256], in0=q, in1=k),
                reads=[self.qk_tm], writes=[self.lrows])
            sl = self.w_tm[:, t * 8:(t + 1) * 8]
            p.step("vector", lambda v, t=t: v.reduce_sum(
                out=self.w_tm[:, t * 8:(t + 1) * 8].rearrange(
                    "p (e o) -> p e o", o=1),
                in_=self.lrows[:, 0:256].rearrange("p (e d) -> p e d", d=cfg.DK),
                axis=AX.X),
                reads=[self.lrows], writes=[self.w_tm])
            p.step("vector", lambda v, sl=sl: v.tensor_scalar(
                out=sl, in0=sl, scalar1=self.ccol("scaleDK"), scalar2=None,
                op0=ALU.mult), reads=[self.w_tm, self.sc], writes=[self.w_tm])
            p.step("vector", lambda v, sl=sl: v.reduce_max(
                out=st[:, 4 * T:4 * T + 1], in_=sl, axis=AX.X),
                reads=[self.w_tm], writes=[st])
            p.step("vector", lambda v, sl=sl: v.tensor_scalar(
                out=sl, in0=sl, scalar1=st[:, 4 * T:4 * T + 1], scalar2=None,
                op0=ALU.subtract), reads=[self.w_tm, st], writes=[self.w_tm])
            p.step("scalar", lambda s, sl=sl: s.activation(
                out=sl, in_=sl, func=AF.Exp),
                reads=[self.w_tm], writes=[self.w_tm])
            p.step("vector", lambda v, sl=sl: v.reduce_sum(
                out=st[:, 4 * T:4 * T + 1], in_=sl, axis=AX.X),
                reads=[self.w_tm], writes=[st])
            p.step("vector", lambda v: v.reciprocal(out=st[:, 4 * T + 1:4 * T + 2],
                                                    in_=st[:, 4 * T:4 * T + 1]),
                   reads=[st], writes=[st])
            p.step("vector", lambda v, sl=sl: v.tensor_scalar(
                out=sl, in0=sl, scalar1=st[:, 4 * T + 1:4 * T + 2], scalar2=None,
                op0=ALU.mult), reads=[self.w_tm, st], writes=[self.w_tm])
        def evA(s, bA, bB):
            for pan in range(cfg.NPAN):
                p0, p1 = pan * 512, min(NT, (pan + 1) * 512)
                bank = (bA, bB)[pan]
                p.step("scalar", lambda sc, bank=bank, p0=p0, p1=p1:
                       sc.copy(out=self.h_fm[:, p0:p1], in_=bank[:, 0:p1 - p0]),
                       reads=[bank], writes=[self.h_fm])
        self.emit_wproj("A1", self.i["w_A1"], 1, evA)
        for t in range(T):
            p.step("tensor", lambda te, t=t: te.transpose(
                out=self.pb[6][0:8, 0:128],
                in_=self.w_tm[:, t * 8:(t + 1) * 8], identity=self.sid_f[:]),
                reads=[self.w_tm, self.sid_f], writes=[self.pb[6]])
            p.step("vector", lambda v, t=t: v.tensor_copy(
                out=self.wT[0:8, t * 128:(t + 1) * 128],
                in_=self.pb[6][0:8, 0:128]),
                reads=[self.pb[6]], writes=[self.wT])
        for pan in range(cfg.NPAN):
            p0, p1 = pan * 512, min(NT, (pan + 1) * 512)
            bank = self.pb[4 + pan]
            p.step("tensor", lambda te, bank=bank, p0=p0, p1=p1: te.matmul(
                out=bank[:, 0:p1 - p0], lhsT=self.ssel[0:cfg.E, :],
                rhs=self.wT[0:cfg.E, p0:p1], start=True, stop=True),
                reads=[self.ssel, self.wT], writes=[bank])
            p.step("vector", lambda v, bank=bank, p0=p0, p1=p1: v.tensor_mul(
                out=self.hp[:, p0:p1], in0=self.h_fm[:, p0:p1],
                in1=bank[:, 0:p1 - p0]),
                reads=[self.h_fm, bank], writes=[self.hp])
        def pre_slice(s):
            stgb = self.bmstg
            bmt = self.bmt[s % 2]
            p.dma_step("sync", None, stgb, stgb[:, 0:128],
                       self.i["w_Bm1"], self.i["w_Bm1"][s, :, :])
            p.step("vector", lambda v, stgb=stgb, bmt=bmt: v.tensor_copy(
                out=bmt[:], in_=stgb[:, 0:128]),
                reads=[stgb], writes=[bmt])

        def extra_mm(te, s, bA, bB):
            last = None
            for pan in range(cfg.NPAN):
                p0, p1 = pan * 512, min(NT, (pan + 1) * 512)
                bank = (bA, bB)[pan]
                last = te.matmul(out=bank[:, 0:p1 - p0],
                                 lhsT=self.bmt[s % 2][:],
                                 rhs=self.hp[:, p0:p1], start=False, stop=True)
            return last
        self.emit_outproj("W1", self.i["w_W1"], self.i["b_W1"], dst="y",
                          extra_mm=extra_mm,
                          extra_reads=[self.bmt[0], self.bmt[1], self.hp],
                          pre_slice=pre_slice)

    def build(self):
        cfg = self.cfg
        self.declare()
        self.load_inputs()
        for blk in cfg.blocks:
            if blk == "l0a":
                first = blk == cfg.blocks[0]
                self.preload_w(self.i["w_qkv0"], 2 * cfg.FC)
                self.emit_ln_to_actT(src=self.i["x"] if first else None)
                if first:
                    for t in range(cfg.T):
                        self.p.dma_step(
                            "scalar", f"xin{t}", self.d["Xc"][t],
                            self.Xap[t * 128:(t + 1) * 128, :],
                            self.i["x"], self.i["x"][t * 128:(t + 1) * 128, :])
                self.emit_qkv(0)
                self.emit_attention(0)
                self.emit_outproj("o0", self.i["w_o0"], self.i["b_o0"])
            elif blk == "l0f":
                self.emit_l0ffn()
            elif blk == "l1a":
                self.emit_ln_to_actT()
                self.emit_qkv(1)
                self.emit_attention(1)
                self.emit_outproj("o1", self.i["w_o1"], None)
            elif blk == "l1f":
                self.emit_l1ffn()
        if "l1f" not in cfg.blocks:
            for t in range(cfg.T):
                tok = self.p.dma_step(
                    "sync", f"yt{t % 4}", self.i["Yt"][t],
                    self.Yap[t * 128:(t + 1) * 128, :],
                    self.d["Xc"][t], self.Xap[t * 128:(t + 1) * 128, :])
                self.final_deps.append(tok)
        return self.p.finalize("sync", self.final_deps)


def prep_inputs(cfg, inputs):
    D, T, FC = cfg.D, cfg.T, cfg.FC
    x = np.asarray(inputs["L0_attn_x"], np.float32)
    B = x.shape[0]
    shared = {}
    shared["w_qkv0"] = tile_w(inputs["L0_attn_attn_qkvw"])
    shared["b_qkv0"] = np.ascontiguousarray(
        np.asarray(inputs["L0_attn_attn_qkvb"], np.float32).reshape(3 * FC, 128).T)
    shared["w_o0"] = tile_w(inputs["L0_attn_attn_ow"])
    shared["b_o0"] = np.ascontiguousarray(
        np.asarray(inputs["L0_attn_attn_ob"], np.float32).reshape(FC, 128).T)
    wres = np.concatenate([np.asarray(inputs["L0_ffn_W_res"], np.float32),
                           np.asarray(inputs["L0_ffn_W_pre"], np.float32),
                           np.asarray(inputs["L0_ffn_W_post"], np.float32)], 0)
    shared["w_res0"] = tile_w(np.ascontiguousarray(wres.T), ow=24)
    shared["bases0"] = np.ascontiguousarray(np.concatenate(
        [np.asarray(inputs["L0_ffn_H_res_base"], np.float32).ravel(),
         np.asarray(inputs["L0_ffn_H_pre_base"], np.float32),
         np.asarray(inputs["L0_ffn_H_post_base"], np.float32)])[:, None])
    shared["gate0"] = np.asarray(inputs["L0_ffn_gate"],
                                 np.float32).reshape(1, 1)
    shared["w_br0"] = tile_w(np.asarray(inputs["L0_ffn_W_branch"], np.float32),
                             fc=cfg.FCC, ow=128)
    shared["w_q1"] = tile_w(np.ascontiguousarray(
        np.asarray(inputs["L1_attn_Wq"], np.float32).T))
    shared["w_k1"] = tile_w(np.ascontiguousarray(
        np.asarray(inputs["L1_attn_Wk"], np.float32).T))
    shared["w_v1"] = tile_w(np.ascontiguousarray(
        np.asarray(inputs["L1_attn_Wv"], np.float32).T))
    shared["w_o1"] = tile_w(np.ascontiguousarray(
        np.asarray(inputs["L1_attn_Wo"], np.float32).T))
    shared["w_W1"] = tile_w(np.ascontiguousarray(
        np.asarray(inputs["L1_ffn_W"], np.float32).T))
    shared["b_W1"] = np.ascontiguousarray(
        np.asarray(inputs["L1_ffn_b"], np.float32).reshape(FC, 128).T)
    wqk = np.concatenate([np.asarray(inputs["L1_ffn_Wq"], np.float32),
                          np.asarray(inputs["L1_ffn_Wk"], np.float32)], 0)
    shared["w_qk1"] = tile_w(np.ascontiguousarray(wqk.T), ow=512)
    shared["w_A1"] = tile_w(np.ascontiguousarray(
        np.asarray(inputs["L1_ffn_A"], np.float32).reshape(cfg.ER, D).T))
    bmpk = np.ascontiguousarray(
        np.asarray(inputs["L1_ffn_Bm"], np.float32).transpose(0, 2, 1).reshape(
            cfg.ER, D))
    shared["w_Bm1"] = tile_w(bmpk, fc=1, ow=128)

    in_maps = []
    for core in range(cfg.n_cores):
        b, par = core // 2, core % 2
        hc = host_constants(cfg, core)
        xb = x[b].reshape(cfg.G, 128, D)
        rows = [2 * t + par for t in range(T)]
        xcore = np.ascontiguousarray(xb[rows].reshape(T * 128, D))
        m = dict(shared)
        m.update(hc)
        m["x"] = xcore
        in_maps.append(m)
    return in_maps


def gather_output(cfg, inputs, results):
    x = np.asarray(inputs["L0_attn_x"], np.float32)
    B, S, D = x.shape
    out = np.empty((B, S, D), np.float32)
    for core in range(cfg.n_cores):
        b, par = core // 2, core % 2
        y = np.asarray(results[core]["y"]).reshape(cfg.T, 128, D)
        for t in range(cfg.T):
            g = 2 * t + par
            out[b, g * 128:(g + 1) * 128, :] = y[t]
    return out


_NC_CACHE = {}


def kernel(**inputs):
    cfg = Cfg()
    key = "full"
    if key not in _NC_CACHE:
        _NC_CACHE[key] = Builder(cfg).build()
    nc = _NC_CACHE[key]
    in_maps = prep_inputs(cfg, inputs)
    res = run_bass_kernel_spmd(nc, in_maps, core_ids=list(range(cfg.n_cores)))
    return gather_output(cfg, inputs, res.results)
